# version 15
# baseline (speedup 1.0000x reference)
import numpy as np
import ml_dtypes
import concourse.bass as bass
import concourse.mybir as mybir

F32 = mybir.dt.float32
BF16 = mybir.dt.bfloat16
F8 = mybir.dt.float8e4
AF = mybir.ActivationFunctionType
OP = mybir.AluOpType
AX = mybir.AxisListType
bfd = ml_dtypes.bfloat16


class Ctr:
    __slots__ = ("name", "sem", "v", "kind")

    def __init__(self, name, kind):
        self.name, self.kind, self.sem, self.v = name, kind, None, 0


class Res:
    __slots__ = ("name", "writer", "readers")

    def __init__(self, name=""):
        self.name, self.writer, self.readers = name, None, []


class Plan:
    ENGINES = ("sync", "gpsimd", "tensor", "vector", "scalar")

    def __init__(self):
        self.ops = {e: [] for e in self.ENGINES}

    def emit(self, eng, fn, reads=(), writes=(), inc=None, extra=()):
        waits = {}

        def add(cv):
            if cv is not None and waits.get(cv[0], -1) < cv[1]:
                waits[cv[0]] = cv[1]

        for r in reads:
            add(r.writer)
        for r in writes:
            add(r.writer)
            for cv in r.readers:
                add(cv)
        if inc is not None and inc.kind == "d":
            waits.pop(inc, None)
        for cv in extra:
            add(cv)
        rec = None
        if inc is not None:
            inc.v += 16 if inc.kind == "d" else 1
            rec = (inc, inc.v)
        self.ops[eng].append((list(waits.items()), fn, rec))
        if rec is not None:
            for r in writes:
                r.writer, r.readers = rec, []
            for r in reads:
                r.readers.append(rec)
        return rec

    def replay(self, eng_name, eng):
        done = {}
        for waits, fn, rec in self.ops[eng_name]:
            for c, v in waits:
                if done.get(c, 0) < v:
                    eng.wait_ge(c.sem, v)
                    done[c] = v
            inst = fn(eng)
            if rec is not None:
                c, _ = rec
                inst.then_inc(c.sem, 16 if c.kind == "d" else 1)


def host_constants(L):
    l = np.arange(L)
    x = (l / L).astype(np.float32)
    ang = (-2.0 * np.pi * x.astype(np.float64)).astype(np.float32)
    om = (np.cos(ang.astype(np.float64)).astype(np.float32)
          + 1j * np.sin(ang.astype(np.float64)).astype(np.float32)).astype(np.complex64)
    one = np.complex64(1.0)
    ratio = ((one - om) / (one + om)).astype(np.complex64)
    T = ratio.imag.astype(np.float32)
    c = (np.complex64(2.0) / (one + om)).astype(np.complex64)
    KC = L // 128
    cre = np.ascontiguousarray(c.real.reshape(KC, 128).T).astype(np.float32)
    cim = np.ascontiguousarray(c.imag.reshape(KC, 128).T).astype(np.float32)
    p = np.arange(128)
    tf = np.arange(128)
    A = 2.0 * np.pi * np.outer(p, tf) / 128.0
    cosA = np.cos(A).astype(np.float32)
    sinA = np.sin(A).astype(np.float32)
    lcol = l.reshape(KC, 128).T
    tc = np.arange(KC)
    B = 2.0 * np.pi * lcol[:, :, None].astype(np.float64) * tc[None, None, :] / L
    cB = (np.cos(B) / L).astype(np.float32).reshape(128, KC * KC)
    sB = (np.sin(B) / L).astype(np.float32).reshape(128, KC * KC)
    return dict(T=T, cre=cre, cim=cim, cosA=cosA, sinA=sinA, cB=cB, sB=sB)


def build(cfg):
    ROWS, D, I = cfg["rows"], cfg["d"], cfg["ffn_i"]
    L, KC, IC = D, D // 128, I // 128
    GR = cfg["group_rows"]
    NG, NM = ROWS // GR, GR // 128
    NPAIR = cfg.get("pair_groups", 2)
    assert NG % NPAIR == 0
    IT = cfg["i_tile"]
    NIT, ITC = I // IT, IT // 128
    HQ = min(512, D)
    NHQ = D // HQ
    W1B = cfg.get("w1_block", 8)
    assert IC % W1B == 0
    NMM = NPAIR * NM
    assert NMM <= 4

    nc = bass.Bass("TRN2", target_bir_lowering=False)

    u_tok = nc.dram_tensor("u_tok", [ROWS, D], F32, kind="ExternalInput")
    u_fm = nc.dram_tensor("u_fm", [D, ROWS], F32, kind="ExternalInput")
    w0n = nc.dram_tensor("w0n", [I, D], BF16, kind="ExternalInput")
    w1n = nc.dram_tensor("w1n", [D, I], BF16, kind="ExternalInput")
    w0dq = nc.dram_tensor("w0dq", [D, I], BF16)
    w1dq = nc.dram_tensor("w1dq", [I, D], BF16)
    b0h = nc.dram_tensor("b0h", [1, I], BF16, kind="ExternalInput")
    b1h = nc.dram_tensor("b1h", [1, D], BF16, kind="ExternalInput")
    sm = {}
    for nm in ("lam_re", "lam_im", "pv_re", "pv_im", "qv_re", "qv_im",
               "ct_re", "ct_im", "bp"):
        sm[nm] = nc.dram_tensor(nm, [64, 1], F32, kind="ExternalInput")
    sm["vct_re"] = nc.dram_tensor("vct_re", [64, 64], F32, kind="ExternalInput")
    sm["vct_im"] = nc.dram_tensor("vct_im", [64, 64], F32, kind="ExternalInput")
    sm["dcoef"] = nc.dram_tensor("dcoef", [1, 1], F32, kind="ExternalInput")
    sm["logstep"] = nc.dram_tensor("logstep", [1, 1], F32, kind="ExternalInput")
    out_h = nc.dram_tensor("out", [ROWS, D], F32, kind="ExternalOutput")
    x1_h = nc.dram_tensor("x1sc", [ROWS, D], BF16)
    kpad_h = nc.dram_tensor("kpad", [1, 127 + L], BF16)

    C = host_constants(L)
    con = {}
    for nm, arr in (("ident", np.eye(128, dtype=bfd)),
                    ("tb", np.broadcast_to(C["T"], (64, L)).copy()),
                    ("cre", C["cre"]), ("cim", C["cim"]),
                    ("cosA", C["cosA"]), ("sinA", C["sinA"]),
                    ("cB", C["cB"]), ("sB", C["sB"]),
                    ("ones_b", np.ones((1, 128), dtype=bfd)),
                    ("ones_f", np.ones((1, 128), dtype=np.float32))):
        con[nm] = nc.inline_tensor(np.ascontiguousarray(arr), name=nm + "_c")

    base = ((nc.sbuf_base + 31) // 32) * 32
    ptr = [base]
    TENS = {}

    def sal(name, shape, dtype, at=None):
        per = int(np.prod(shape[1:])) * mybir.dt.size(dtype)
        if at is None:
            at = ptr[0]
            ptr[0] = ((at + per + 31) // 32) * 32
            assert ptr[0] <= nc.sbuf_top, f"SBUF overflow at {name}: {ptr[0]} > {nc.sbuf_top}"
        h = nc.alloc_sbuf_tensor_at(name, shape, dtype, offset=at)
        TENS[name] = (h, at, per)
        return h

    Mt = sal("Mt", [128, L], BF16)
    Xd = sal("Xd", [128, KC * GR], BF16)
    W0b = sal("W0b", [128, 2 * KC * IT], BF16)
    NH2 = 2 if NHQ >= 2 else 1
    W1b = sal("W1b", [128, 2 * W1B * NH2 * HQ], BF16)
    X1rb = sal("X1rb", [128, D], BF16)
    b0b = sal("b0b", [1, 2 * IT], BF16)
    b1b = sal("b1b", [1, HQ], BF16)
    ident = sal("ident", [128, 128], BF16)
    onesb = sal("onesb", [1, 128], BF16)
    onesf = sal("onesf", [1, 128], F32)
    cret = sal("cret", [128, KC], F32)
    cimt = sal("cimt", [128, KC], F32)
    Rt = sal("Rt", [128, KC], F32)
    It_ = sal("It_", [128, KC], F32)
    ktileb = sal("ktileb", [128, KC], BF16)
    zpref = sal("zpref", [1, 128], BF16)
    sv = {}
    for nm in ("lam_re", "lam_im", "pv_re", "pv_im", "qv_re", "qv_im", "ct_re",
               "ct_im", "bp", "bre", "bim", "neglamre", "lamre2", "s64"):
        sv[nm] = sal("sv_" + nm, [64, 1], F32)
    opd = sal("opd", [128, 1], F32)
    V1 = sal("V1", [64, 8], F32)
    V2 = sal("V2", [64, 8], F32)
    sc1 = sal("sc1", [1, 1], F32)
    sc2 = sal("sc2", [1, 1], F32)
    vst = sal("vst", [64, 2], F32)
    vsums = sal("vsums", [128, 2 * KC], F32)
    vsq = sal("vsq", [128, 2 * KC], F32)
    sqscr = sal("sqscr", [128, 128], BF16)
    lnm = sal("lnm", [128, 8], F32)
    lnr = sal("lnr", [128, 8], F32)
    lnt = sal("lnt", [128, 8], F32)
    qa = sal("qa", [128, 8], F32)
    qs = sal("qs", [128, 8], F32)
    qr = sal("qr", [128, 8], F32)
    l2s = sal("l2s", [128, 4], F32)
    l2q = sal("l2q", [128, 8 * 4], F32)
    l2m = sal("l2m", [128, 4], F32)
    l2r = sal("l2r", [128, 4], F32)
    q8t = sal("q8t", [128, 2 * 128], F8)
    qdt = sal("qdt", [128, 2 * 128], BF16)
    dgt = sal("dgt", [128, 2 * IT], BF16)
    scrD = sal("scrD", [128, D], BF16)
    DdA = sal("DdA", [128, IC * GR], BF16)
    DdB = sal("DdB", [128, IC * GR], BF16)
    atA, atB = TENS["DdA"][1], TENS["DdB"][1]
    arena_bytes = 2 * IC * GR * 2

    Esb = sal("Esb", [128, NMM * D], BF16,
              at=(TENS["W0b"][1] if NMM * D * 2 <= TENS["W0b"][2] else None))
    WQK0 = min(2048, D)
    WQK1 = min(2048, I)
    WQS = max(WQK0, WQK1)
    _need = WQS * 2 + WQS + WQS * 2 + 2 * 2048 + 512
    _fit = _need <= TENS["W1b"][2]
    _o = [TENS["W1b"][1] if _fit else None]
    qesc = []

    def _qal(name, shape, dtype):
        n = int(np.prod(shape[1:])) * mybir.dt.size(dtype)
        at = _o[0]
        h = sal(name, shape, dtype, at=at)
        if at is not None:
            _o[0] = at + n
        return h

    wq_in = _qal("wq_in", [128, WQS], BF16)
    wq_q8 = _qal("wq_q8", [128, WQS], F8)
    wq_out = _qal("wq_out", [128, WQS], BF16)
    wq_stg = _qal("wq_stg", [128, 2 * 1024], BF16)
    wq_am = _qal("wq_am", [128, 16], F32)
    wq_s = _qal("wq_s", [128, 16], F32)
    wq_r = _qal("wq_r", [128, 16], F32)
    v2t = sal("v2t", [128, D], F32,
              at=(TENS["Xd"][1] if D * 4 <= TENS["Xd"][2] else None))

    scr_fit = (KC * GR * 2 + 2 * D * 4 + 4 * 128 * 4) <= IC * GR * 2
    conv_scr = []
    for h, at0 in (("A", atA), ("B", atB)):
        o = at0 if scr_fit else None
        def nxt(o, n):
            return (None, None) if o is None else (o, o + n)
        a, o = nxt(o, KC * GR * 2)
        ut = sal("Ut" + h, [128, KC * GR], BF16, at=a)
        a, o = nxt(o, D * 4)
        vt0 = sal("vt0" + h, [128, D], F32, at=a)
        a, o = nxt(o, D * 4)
        vt1 = sal("vt1" + h, [128, D], F32, at=a)
        a, o = nxt(o, 4 * 128 * 4)
        uj = sal("uj" + h, [128, 4 * 128], F32, at=a)
        conv_scr.append((ut, (vt0, vt1), uj))

    pho = [atA]
    PH0_BYTES = (5 * L * 4 + 5 * KC * KC * 4 + 2 * 128 * 4 + 2 * 64 * 4
                 + KC * 8 * 4 + KC * 10 * 4)

    def ph(name, shape, dtype):
        n = int(np.prod(shape[1:])) * mybir.dt.size(dtype)
        if PH0_BYTES <= arena_bytes:
            h = sal(name, shape, dtype, at=pho[0])
            pho[0] += n
        else:
            h = sal(name, shape, dtype)
        return h

    Tb = ph("Tb", [64, L], F32)
    dimt = ph("dimt", [64, L], F32)
    d2t = ph("d2t", [64, L], F32)
    rCt = ph("rCt", [128, KC * KC], F32)
    rSt = ph("rSt", [128, KC * KC], F32)
    rS2 = ph("rS2", [128, KC * KC], F32)
    cbt = ph("cbt", [128, KC * KC], F32)
    sbt = ph("sbt", [128, KC * KC], F32)
    cosA_s = ph("cosA_s", [128, 128], F32)
    sinA_s = ph("sinA_s", [128, 128], F32)
    vctre_s = ph("vctre_s", [64, 64], F32)
    vctim_s = ph("vctim_s", [64, 64], F32)
    kall = ph("kall", [128, KC * 8], F32)
    ars = ph("ars", [128, KC * 10], F32)
    dreinv = ph("dreinv", [64, L], F32)
    diminv = ph("diminv", [64, L], F32)

    psb = [nc.alloc_psum_tensor(f"psb{i}", [128, 512], F32) for i in range(8)]

    plan = Plan()
    ctrs = []

    def CT(name, kind):
        c = Ctr(name, kind)
        ctrs.append(c)
        return c

    pe, dv, ac, po = CT("pe", "c"), CT("dv", "c"), CT("ac", "c"), CT("po", "c")
    qc = CT("qc", "d")
    qm = CT("qm", "d")
    qsm = CT("qsm", "d")
    qkp = CT("qkp", "d")
    qw0 = [CT("qw0a", "d"), CT("qw0b", "d")]
    qw1 = [CT("qw1a", "d"), CT("qw1b", "d")]
    qb0 = [CT("qb0a", "d"), CT("qb0b", "d")]
    qb1 = CT("qb1", "d")
    qut = [CT("quta", "d"), CT("qutb", "d")]
    quj = [CT(f"quj{i}", "d") for i in range(4)]
    qx1w = [CT("qx1wa", "d"), CT("qx1wb", "d")]
    qx1r = CT("qx1r", "d")
    qout = [CT("qouta", "d"), CT("qoutb", "d")]
    qqi = [CT("qqia", "d"), CT("qqib", "d")]
    qqs = [CT(f"qqs{i}", "d") for i in range(4)]

    r_const = Res("const")
    r_small = Res("small")
    r_M = Res("M")
    r_kpad = Res("kpad")
    r_ps = [Res(f"ps{i}") for i in range(8)]
    r_pst = [Res(f"pst{i}") for i in range(8)]
    r_W0 = [Res("W0a"), Res("W0b")]
    r_W1 = [Res("W1a"), Res("W1b")]
    r_b0 = [Res("b0a"), Res("b0b")]
    r_b1 = Res("b1")
    r_Ut = [Res("UtA"), Res("UtB")]
    r_vt = [[Res(f"vt{h}{s}") for s in range(2)] for h in range(2)]
    r_uj = [Res(f"uj{i}") for i in range(4)]
    r_Xd = Res("Xd")
    r_Dd = [Res("DdA"), Res("DdB")]
    r_ph0 = Res("ph0")
    r_kall = Res("kall")
    r_ar = Res("ar")
    r_kt = Res("ktile")
    r_zp = Res("zpref")
    r_x1h = [Res("x1hbm0"), Res("x1hbm1")]
    r_x1rb = Res("x1rb")
    r_scrD = Res("scrD")
    r_q8 = [Res(f"q8_{i}") for i in range(2)]
    r_qd = [Res(f"qd_{i}") for i in range(2)]
    r_dg = [Res(f"dg_{i}") for i in range(2)]
    r_Esb = [Res(f"E{m}") for m in range(NMM)]
    r_lns = Res("lns")
    r_qsc = Res("qsc")
    r_vsums = [Res("vs0"), Res("vs1")]
    r_sqscr = Res("sqscr")
    r_outh = [Res("outh0"), Res("outh1")]
    r_wqin = Res("wqin")
    r_wq8 = Res("wq8")
    r_wqout = Res("wqout")
    r_wqsc = Res("wqsc")
    r_wqstg = [Res("wqstg0"), Res("wqstg1")]
    r_psq = [Res(f"psq{i}") for i in range(8)]
    r_opd = Res("opd")
    r_sc = Res("sc")
    r_l2 = Res("l2")

    E = plan.emit

    def dma(engq, q, outap, inap, reads=(), writes=(), extra=()):
        return E(engq, lambda e: e.dma_start(outap, inap),
                 reads=reads, writes=writes, inc=q, extra=extra)

    def fence():
        return [(dv, dv.v), (ac, ac.v)]

    wq_units = []
    for ib in range(I // 128):
        for ks in range(D // WQK0):
            wq_units.append(("w0", ib, ks))
    for hb in range(D // 128):
        for cs in range(I // WQK1):
            wq_units.append(("w1", hb, cs))
    w0rdy = [[None] * (D // WQK0) for _ in range(I // 128)]
    w1rdy = [[None] * (I // WQK1) for _ in range(D // 128)]
    wq_pos = [0]
    wq_store_ct = [0]

    def wq_step():
        if wq_pos[0] >= len(wq_units):
            return
        kind, blk, span = wq_units[wq_pos[0]]
        wq_pos[0] += 1
        W = WQK0 if kind == "w0" else WQK1
        NCH = W // 128
        src_h = w0n if kind == "w0" else w1n
        dst_h = w0dq if kind == "w0" else w1dq
        rowlen = D if kind == "w0" else I
        outrow = I if kind == "w0" else D
        off = blk * 128 * rowlen + span * W
        inap = bass.AP(src_h, off, [[rowlen, 128], [1, W]])
        int_ = wq_in[:, 0:W]
        am_, s_, r_ = wq_am[:, 0:NCH], wq_s[:, 0:NCH], wq_r[:, 0:NCH]
        dma("sync", qqi[wq_pos[0] % 2], int_, inap, writes=[r_wqin])
        in3 = bass.AP(wq_in, 0, [[WQS, 128], [128, NCH], [1, 128]])
        E("vector", lambda e: e.tensor_reduce(am_, in3, AX.X, OP.max,
                                              apply_absolute_value=True),
          reads=[r_wqin], writes=[r_wqsc], inc=dv)
        E("vector", lambda e: e.tensor_scalar(s_, am_, 1e-4, 1.0 / 224.0, OP.max, OP.mult),
          reads=[r_wqsc], writes=[r_wqsc], inc=dv)
        E("vector", lambda e: e.reciprocal(r_, s_), reads=[r_wqsc], writes=[r_wqsc], inc=dv)
        rb3 = bass.AP(wq_r, 0, [[16, 128], [1, NCH], [0, 128]])
        sb3 = bass.AP(wq_s, 0, [[16, 128], [1, NCH], [0, 128]])
        q83 = bass.AP(wq_q8, 0, [[WQS, 128], [128, NCH], [1, 128]])
        o3 = bass.AP(wq_out, 0, [[WQS, 128], [128, NCH], [1, 128]])
        E("vector", lambda e: e.tensor_tensor(q83, in3, rb3, OP.mult),
          reads=[r_wqin, r_wqsc], writes=[r_wq8], inc=dv)
        for c in range(NCH):
            if c % 2 == 0:
                E("vector", lambda e, c=c: e.tensor_scalar(
                    wq_out[:, c * 128:(c + 1) * 128], wq_q8[:, c * 128:(c + 1) * 128],
                    wq_s[:, c:c + 1], None, OP.mult),
                  reads=[r_wq8, r_wqsc], writes=[r_wqout], inc=dv)
            else:
                E("scalar", lambda e, c=c: e.activation(
                    wq_out[:, c * 128:(c + 1) * 128], wq_q8[:, c * 128:(c + 1) * 128],
                    AF.Copy, scale=wq_s[:, c:c + 1]),
                  reads=[r_wq8, r_wqsc], writes=[r_wqout], inc=ac)
        nbatch = (NCH + 7) // 8
        recs = []
        for b in range(nbatch):
            j0, j1 = b * 8, min(b * 8 + 8, NCH)
            ssl = b % 2
            for j in range(j0, j1):
                pv = psb[7][:, :].bitcast(BF16)[:, (j - j0) * 128:(j - j0 + 1) * 128]
                E("tensor", lambda e, j=j, pv=pv: e.transpose(
                    pv, wq_out[:, j * 128:(j + 1) * 128], ident[:, :]),
                  reads=[r_wqout, r_const], writes=[r_psq[j - j0]], inc=pe)
            nb = (j1 - j0) * 128
            pall = psb[7][:, :].bitcast(BF16)[:, 0:nb]
            if b % 2 == 0:
                E("vector", lambda e, pall=pall, ssl=ssl, nb=nb:
                  e.tensor_copy(wq_stg[:, ssl * 1024:ssl * 1024 + nb], pall),
                  reads=[r_psq[j] for j in range(j1 - j0)], writes=[r_wqstg[ssl]], inc=dv)
            else:
                E("scalar", lambda e, pall=pall, ssl=ssl, nb=nb:
                  e.activation(wq_stg[:, ssl * 1024:ssl * 1024 + nb], pall, AF.Copy),
                  reads=[r_psq[j] for j in range(j1 - j0)], writes=[r_wqstg[ssl]], inc=ac)
            ssl_s = wq_store_ct[0] % 4
            prev = wq_store_ct[0] - 4
            ex = [(qqs[ssl_s], 16 * (prev // 4 + 1))] if prev >= 0 else []
            wq_store_ct[0] += 1
            stoff = (span * W + j0 * 128) * outrow + blk * 128
            stap = bass.AP(dst_h, stoff,
                           [[outrow, 128], [128 * outrow, j1 - j0], [1, 128]])
            recs.append(dma("gpsimd", qqs[ssl_s], stap,
                            wq_stg[:, ssl * 1024:ssl * 1024 + nb], reads=[r_wqstg[ssl]],
                            extra=ex))
        if kind == "w0":
            w0rdy[blk][span] = recs
        else:
            w1rdy[blk][span] = recs

    def pump(n):
        for _ in range(n):
            wq_step()

    dma("sync", qc, ident[:, :], con["ident"].ap(), writes=[r_const])
    dma("sync", qc, onesb[:, :], con["ones_b"].ap(), writes=[r_const])
    dma("sync", qc, onesf[:, :], con["ones_f"].ap(), writes=[r_const])
    dma("sync", qc, Tb[:, :], con["tb"].ap(), writes=[r_ph0])
    dma("sync", qc, cret[:, :], con["cre"].ap(), writes=[r_const])
    dma("sync", qc, cimt[:, :], con["cim"].ap(), writes=[r_const])
    dma("sync", qc, cbt[:, :], con["cB"].ap(), writes=[r_ph0])
    dma("sync", qc, sbt[:, :], con["sB"].ap(), writes=[r_ph0])
    dma("sync", qc, cosA_s[:, :], con["cosA"].ap(), writes=[r_ph0])
    dma("sync", qc, sinA_s[:, :], con["sinA"].ap(), writes=[r_ph0])
    for nm in ("lam_re", "lam_im", "pv_re", "pv_im", "qv_re", "qv_im",
               "ct_re", "ct_im", "bp"):
        dma("gpsimd", qsm, sv[nm][:, :], sm[nm].ap(), writes=[r_small])
    dma("gpsimd", qsm, vctre_s[:, :], sm["vct_re"].ap(), writes=[r_small])
    dma("gpsimd", qsm, vctim_s[:, :], sm["vct_im"].ap(), writes=[r_small])
    dma("gpsimd", qsm, sc1[:, :], sm["logstep"].ap(), writes=[r_sc])
    dma("gpsimd", qsm, opd[0:1, 0:1], sm["dcoef"].ap(), writes=[r_opd])

    r_const.writer = (qc, qc.v)
    r_ph0.writer = (qc, qc.v)
    r_small.writer = (qsm, qsm.v)
    r_sc.writer = (qsm, qsm.v)
    r_opd.writer = (qsm, qsm.v)

    E("scalar", lambda e: e.activation(sc2[0:1, :], sc1[0:1, :], AF.Exp),
      reads=[r_sc], writes=[r_sc], inc=ac)
    E("vector", lambda e: e.reciprocal(sc1[0:1, :], sc2[0:1, :]),
      reads=[r_sc], writes=[r_sc], inc=dv)
    E("vector", lambda e: e.tensor_scalar(sc2[0:1, :], sc1[0:1, :], 2.0, None, OP.mult),
      reads=[r_sc], writes=[r_sc], inc=dv)
    E("tensor", lambda e: e.matmul(psb[6][0:64, 0:1], onesf[0:1, 0:64], sc2[0:1, :],
                                   start=True, stop=True),
      reads=[r_sc, r_const], writes=[r_ps[6]], inc=pe)
    E("vector", lambda e: e.tensor_copy(sv["s64"][:, :], psb[6][0:64, 0:1]),
      reads=[r_ps[6]], writes=[r_small], inc=dv)
    E("tensor", lambda e: e.matmul(psb[6][:, 8:9], onesf[0:1, 0:128], opd[0:1, 0:1],
                                   start=True, stop=True),
      reads=[r_opd, r_const], writes=[r_ps[6]], inc=pe)
    E("vector", lambda e: e.tensor_scalar(opd[:, :], psb[6][:, 8:9], 1.0, None, OP.add),
      reads=[r_ps[6]], writes=[r_opd], inc=dv)

    E("tensor", lambda e: e.matmul(psb[6][0:64, 0:1], vctre_s[:, :], sv["bp"][:, :],
                                   start=True, stop=True),
      reads=[r_small, r_ph0], writes=[r_ps[6]], inc=pe)
    E("vector", lambda e: e.tensor_copy(sv["bre"][:, :], psb[6][0:64, 0:1]),
      reads=[r_ps[6]], writes=[r_small], inc=dv)
    E("tensor", lambda e: e.matmul(psb[6][0:64, 0:1], vctim_s[:, :], sv["bp"][:, :],
                                   start=True, stop=True),
      reads=[r_small, r_ph0], writes=[r_ps[6]], inc=pe)
    E("vector", lambda e: e.tensor_copy(sv["bim"][:, :], psb[6][0:64, 0:1]),
      reads=[r_ps[6]], writes=[r_small], inc=dv)
    E("vector", lambda e: e.tensor_scalar(sv["neglamre"][:, :], sv["lam_re"][:, :], -1.0, None, OP.mult),
      reads=[r_small], writes=[r_small], inc=dv)
    E("vector", lambda e: e.tensor_mul(sv["lamre2"][:, :], sv["lam_re"][:, :], sv["lam_re"][:, :]),
      reads=[r_small], writes=[r_small], inc=dv)

    def vcomb(col, a1, b1_, a2, b2_, sgn, also_neg_to=None):
        E("vector", lambda e: e.tensor_mul(vst[:, 0:1], sv[a1][:, :], sv[b1_][:, :]),
          reads=[r_small], writes=[r_lns], inc=dv)
        E("vector", lambda e: e.tensor_mul(vst[:, 1:2], sv[a2][:, :], sv[b2_][:, :]),
          reads=[r_small], writes=[r_lns], inc=dv)
        op = OP.add if sgn > 0 else OP.subtract
        E("vector", lambda e: e.tensor_tensor(V1[:, col:col + 1], vst[:, 0:1], vst[:, 1:2], op),
          reads=[r_lns], writes=[r_small], inc=dv)
        if also_neg_to is not None:
            E("vector", lambda e: e.tensor_scalar(V2[:, also_neg_to:also_neg_to + 1],
                                                  V1[:, col:col + 1], -1.0, None, OP.mult),
              reads=[r_small], writes=[r_small], inc=dv)

    def vcopy(dst_col, src_col):
        E("vector", lambda e: e.tensor_copy(V2[:, dst_col:dst_col + 1], V1[:, src_col:src_col + 1]),
          reads=[r_small], writes=[r_small], inc=dv)

    vcomb(0, "ct_re", "bre", "ct_im", "bim", +1, also_neg_to=1)
    vcomb(1, "ct_re", "bim", "ct_im", "bre", -1)
    vcopy(0, 1)
    vcomb(2, "ct_re", "pv_re", "ct_im", "pv_im", +1, also_neg_to=3)
    vcomb(3, "ct_re", "pv_im", "ct_im", "pv_re", -1)
    vcopy(2, 3)
    vcomb(4, "qv_re", "bre", "qv_im", "bim", +1, also_neg_to=5)
    vcomb(5, "qv_re", "bim", "qv_im", "bre", -1)
    vcopy(4, 5)
    vcomb(6, "qv_re", "pv_re", "qv_im", "pv_im", +1, also_neg_to=7)
    vcomb(7, "qv_re", "pv_im", "qv_im", "pv_re", -1)
    vcopy(6, 7)

    E("vector", lambda e: e.tensor_scalar(dimt[:, :], Tb[:, :], sv["s64"][:, :], None, OP.mult),
      reads=[r_ph0, r_small], writes=[r_ph0], inc=dv)
    E("vector", lambda e: e.tensor_scalar(dimt[:, :], dimt[:, :], sv["lam_im"][:, :], None, OP.subtract),
      reads=[r_small, r_ph0], writes=[r_ph0], inc=dv)
    E("vector", lambda e: e.tensor_mul(d2t[:, :], dimt[:, :], dimt[:, :]),
      reads=[r_ph0], writes=[r_ph0], inc=dv)
    E("vector", lambda e: e.tensor_scalar(d2t[:, :], d2t[:, :], sv["lamre2"][:, :], None, OP.add),
      reads=[r_small, r_ph0], writes=[r_ph0], inc=dv)
    E("vector", lambda e: e.reciprocal(d2t[:, :], d2t[:, :]),
      reads=[r_ph0], writes=[r_ph0], inc=dv)
    E("vector", lambda e: e.tensor_scalar(dreinv[:, :], d2t[:, :], sv["neglamre"][:, :], None, OP.mult),
      reads=[r_small, r_ph0], writes=[r_ph0], inc=dv)
    E("vector", lambda e: e.tensor_mul(diminv[:, :], dimt[:, :], d2t[:, :]),
      reads=[r_ph0], writes=[r_ph0], inc=dv)

    for cc in range(KC):
        E("tensor", lambda e, cc=cc: e.matmul(psb[6][:, 0:8],
                                              dreinv[:, cc * 128:(cc + 1) * 128],
                                              V1[:, :], start=True, stop=False),
          reads=[r_ph0, r_small], writes=[r_ps[6]])
        E("tensor", lambda e, cc=cc: e.matmul(psb[6][:, 0:8],
                                              diminv[:, cc * 128:(cc + 1) * 128],
                                              V2[:, :], start=False, stop=True),
          reads=[r_ph0, r_small], writes=[r_ps[6]], inc=pe)
        E("vector", lambda e, cc=cc: e.tensor_copy(kall[:, 8 * cc:8 * cc + 8], psb[6][:, 0:8]),
          reads=[r_ps[6]], writes=[r_kall], inc=dv)

    def kv(i):
        return bass.AP(kall, i, [[KC * 8, 128], [8, KC]])

    def scr(i):
        return ars[:, i * KC:(i + 1) * KC]

    def V(fn, reads, writes):
        E("vector", fn, reads=reads, writes=writes, inc=dv)

    V(lambda e: e.tensor_scalar(scr(0), kv(6), 1.0, None, OP.add), [r_kall], [r_ar])
    V(lambda e: e.tensor_mul(scr(1), scr(0), scr(0)), [r_ar], [r_ar])
    V(lambda e: e.tensor_mul(scr(2), kv(7), kv(7)), [r_kall], [r_ar])
    V(lambda e: e.tensor_add(scr(1), scr(1), scr(2)), [r_ar], [r_ar])
    V(lambda e: e.reciprocal(scr(1), scr(1)), [r_ar], [r_ar])
    V(lambda e: e.tensor_mul(scr(2), kv(2), kv(4)), [r_kall], [r_ar])
    V(lambda e: e.tensor_mul(scr(3), kv(3), kv(5)), [r_kall], [r_ar])
    V(lambda e: e.tensor_sub(scr(2), scr(2), scr(3)), [r_ar], [r_ar])
    V(lambda e: e.tensor_mul(scr(3), kv(2), kv(5)), [r_kall], [r_ar])
    V(lambda e: e.tensor_mul(scr(4), kv(3), kv(4)), [r_kall], [r_ar])
    V(lambda e: e.tensor_add(scr(3), scr(3), scr(4)), [r_ar], [r_ar])
    V(lambda e: e.tensor_mul(scr(4), scr(2), scr(0)), [r_ar], [r_ar])
    V(lambda e: e.tensor_mul(scr(5), scr(3), kv(7)), [r_ar, r_kall], [r_ar])
    V(lambda e: e.tensor_add(scr(4), scr(4), scr(5)), [r_ar], [r_ar])
    V(lambda e: e.tensor_mul(scr(4), scr(4), scr(1)), [r_ar], [r_ar])
    V(lambda e: e.tensor_mul(scr(5), scr(3), scr(0)), [r_ar], [r_ar])
    V(lambda e: e.tensor_mul(scr(6), scr(2), kv(7)), [r_ar, r_kall], [r_ar])
    V(lambda e: e.tensor_sub(scr(5), scr(5), scr(6)), [r_ar], [r_ar])
    V(lambda e: e.tensor_mul(scr(5), scr(5), scr(1)), [r_ar], [r_ar])
    V(lambda e: e.tensor_sub(scr(6), kv(0), scr(4)), [r_kall, r_ar], [r_ar])
    V(lambda e: e.tensor_sub(scr(7), kv(1), scr(5)), [r_kall, r_ar], [r_ar])
    V(lambda e: e.tensor_mul(scr(8), cret[:, :], scr(6)), [r_const, r_ar], [r_ar])
    V(lambda e: e.tensor_mul(scr(9), cimt[:, :], scr(7)), [r_const, r_ar], [r_ar])
    V(lambda e: e.tensor_sub(Rt[:, :], scr(8), scr(9)), [r_ar], [r_ar])
    V(lambda e: e.tensor_mul(scr(8), cret[:, :], scr(7)), [r_const, r_ar], [r_ar])
    V(lambda e: e.tensor_mul(scr(9), cimt[:, :], scr(6)), [r_const, r_ar], [r_ar])
    V(lambda e: e.tensor_add(It_[:, :], scr(8), scr(9)), [r_ar], [r_ar])

    Rexp = bass.AP(Rt, 0, [[KC, 128], [1, KC], [0, KC]])
    Iexp = bass.AP(It_, 0, [[KC, 128], [1, KC], [0, KC]])
    V(lambda e: e.tensor_mul(rCt[:, :], cbt[:, :], Rexp), [r_ph0, r_ar], [r_ph0])
    V(lambda e: e.tensor_mul(rSt[:, :], sbt[:, :], Iexp), [r_ph0, r_ar], [r_ph0])
    V(lambda e: e.tensor_sub(rCt[:, :], rCt[:, :], rSt[:, :]), [r_ph0], [r_ph0])
    V(lambda e: e.tensor_mul(rSt[:, :], sbt[:, :], Rexp), [r_ph0, r_ar], [r_ph0])
    V(lambda e: e.tensor_mul(rS2[:, :], cbt[:, :], Iexp), [r_ph0, r_ar], [r_ph0])
    V(lambda e: e.tensor_add(rSt[:, :], rSt[:, :], rS2[:, :]), [r_ph0], [r_ph0])
    V(lambda e: e.tensor_scalar(rSt[:, :], rSt[:, :], -1.0, None, OP.mult), [r_ph0], [r_ph0])

    for cc in range(KC):
        E("tensor", lambda e, cc=cc: e.matmul(psb[5][:, 0:KC], cosA_s[:, :],
                                              rCt[:, cc * KC:(cc + 1) * KC],
                                              start=(cc == 0), stop=False),
          reads=[r_ph0], writes=[r_ps[5]])
        E("tensor", lambda e, cc=cc: e.matmul(psb[5][:, 0:KC], sinA_s[:, :],
                                              rSt[:, cc * KC:(cc + 1) * KC],
                                              start=False, stop=(cc == KC - 1)),
          reads=[r_ph0], writes=[r_ps[5]], inc=(pe if cc == KC - 1 else None))
    E("scalar", lambda e: e.activation(ktileb[:, :], psb[5][:, 0:KC], AF.Copy),
      reads=[r_ps[5]], writes=[r_kt], inc=ac)
    E("gpsimd", lambda e: e.memset(zpref[0:1, 0:127], 0.0), writes=[r_zp], inc=po)
    dma("gpsimd", qkp, bass.AP(kpad_h, 0, [[127, 1], [1, 127]]), zpref[0:1, 0:127],
        reads=[r_zp], writes=[r_kpad])
    dma("gpsimd", qkp, bass.AP(kpad_h, 127, [[KC, 128], [1, KC]]), ktileb[:, :],
        reads=[r_kt], writes=[r_kpad])
    for b in range(128):
        dma("sync", qm, Mt[b:b + 1, :], bass.AP(kpad_h, 127 - b, [[1, 1], [1, L]]),
            reads=[r_kpad], writes=[r_M])
    ph0_done = [(pe, pe.v), (dv, dv.v), (ac, ac.v), (po, po.v)]
    pump(6)

    q8slot = [0]

    def emit_qd(src_ap, src_res, qcol, dst_ap, dst_res):
        s = q8slot[0]
        q8slot[0] = (s + 1) % 2
        E("vector", lambda e: e.reduce_max(qa[:, qcol:qcol + 1], src_ap, axis=AX.X,
                                           apply_absolute_value=True),
          reads=[src_res], writes=[r_qsc], inc=dv)
        E("vector", lambda e: e.tensor_scalar(qs[:, qcol:qcol + 1], qa[:, qcol:qcol + 1],
                                              1e-4, 1.0 / 224.0, OP.max, OP.mult),
          reads=[r_qsc], writes=[r_qsc], inc=dv)
        E("vector", lambda e: e.reciprocal(qr[:, qcol:qcol + 1], qs[:, qcol:qcol + 1]),
          reads=[r_qsc], writes=[r_qsc], inc=dv)
        E("vector", lambda e, s=s: e.tensor_scalar(q8t[:, s * 128:(s + 1) * 128], src_ap,
                                                   qr[:, qcol:qcol + 1], None, OP.mult),
          reads=[src_res, r_qsc], writes=[r_q8[s]], inc=dv)
        E("scalar", lambda e, s=s: e.activation(dst_ap, q8t[:, s * 128:(s + 1) * 128],
                                                AF.Copy, scale=qs[:, qcol:qcol + 1]),
          reads=[r_q8[s], r_qsc], writes=[dst_res], inc=ac)

    tslot = [0]

    def emit_transpose(src_ap, src_res, dst_ap, dst_res, extra=()):
        s = tslot[0]
        tslot[0] = (s + 1) % 8
        pview = psb[6][:, :].bitcast(BF16)[:, s * 128:(s + 1) * 128]
        E("tensor", lambda e: e.transpose(pview, src_ap, ident[:, :]),
          reads=[src_res, r_const], writes=[r_pst[s]], inc=pe, extra=extra)
        E("vector", lambda e: e.tensor_copy(dst_ap, pview),
          reads=[r_pst[s]], writes=[dst_res], inc=dv)

    def conv_group(g):
        half = g % 2
        Ut, vts, ujs = conv_scr[half]
        r0 = g * GR
        pe_fence = fence()
        dd_guard = ([r_Dd[half].writer] if r_Dd[half].writer else []) \
            + list(r_Dd[half].readers) + list(ph0_done)
        dma("gpsimd", qut[half], Ut[:, :].rearrange("p (c n) -> p c n", c=KC),
            bass.AP(u_fm, r0, [[ROWS, 128], [128 * ROWS, KC], [1, GR]]),
            writes=[r_Ut[half]], extra=dd_guard)
        for m in range(NM):
            mslot = m % 2
            rvt = r_vt[half][mslot]
            rvs = r_vsums[mslot]
            for w in range((KC + 2) // 3):
                jlo, jhi = 3 * w, min(3 * w + 3, KC)
                bank0 = 3 * (w % 2)
                bres = [r_ps[bank0 + (j - jlo)] for j in range(jlo, jhi)]
                first_of_group = (m == 0 and w == 0)
                for j in range(jlo, jhi):
                    us = j % 4
                    dma("sync", quj[us], ujs[:, us * 128:(us + 1) * 128],
                        bass.AP(u_tok, (r0 + m * 128) * D + j * 128, [[D, 128], [1, 128]]),
                        writes=[r_uj[us]], extra=dd_guard)
                for i in range(jhi):
                    for j in range(max(i, jlo), jhi):
                        bi = j - jlo
                        E("tensor",
                          lambda e, i=i, j=j, m=m, bank0=bank0, bi=bi, Ut=Ut: e.matmul(
                              psb[bank0 + bi][:, 0:128],
                              Ut[:, i * GR + m * 128:i * GR + (m + 1) * 128],
                              Mt[:, (j - i) * 128:(j - i + 1) * 128],
                              start=(i == 0), stop=(i == j)),
                          reads=[r_Ut[half], r_M],
                          writes=[bres[bi]],
                          inc=(pe if i == j else None),
                          extra=(pe_fence if first_of_group and i == 0 else ()))
                pump(1)
                for j in range(jlo, jhi):
                    bi = j - jlo
                    us = j % 4
                    E("vector", lambda e, j=j, bi=bi, bank0=bank0, us=us, vt=vts[mslot], mslot=mslot, ujs=ujs: e.scalar_tensor_tensor(
                        vt[:, j * 128:(j + 1) * 128], ujs[:, us * 128:(us + 1) * 128],
                        opd[:, :], psb[bank0 + bi][:, 0:128], OP.mult, OP.add,
                        accum_out=vsums[:, mslot * KC + j:mslot * KC + j + 1]),
                      reads=[r_uj[us], r_opd, bres[bi]], writes=[rvt, rvs], inc=dv)
                    E("scalar", lambda e, j=j, vt=vts[mslot], mslot=mslot: e.activation(
                        sqscr[:, :], vt[:, j * 128:(j + 1) * 128], AF.Square,
                        accum_out=vsq[:, mslot * KC + j:mslot * KC + j + 1]),
                      reads=[rvt], writes=[r_sqscr, rvs], inc=ac)
        for m in range(NM):
            mslot = m % 2
            rvt = r_vt[half][mslot]
            rvs = r_vsums[mslot]
            vt = vts[mslot]
            E("vector", lambda e, mslot=mslot: e.tensor_reduce(
                lnt[:, mslot:mslot + 1], vsums[:, mslot * KC:(mslot + 1) * KC], AX.X, OP.add),
              reads=[rvs], writes=[r_lns], inc=dv)
            E("vector", lambda e, mslot=mslot: e.tensor_scalar(
                lnm[:, mslot:mslot + 1], lnt[:, mslot:mslot + 1], 1.0 / D, None, OP.mult),
              reads=[r_lns], writes=[r_lns], inc=dv)
            E("vector", lambda e, mslot=mslot: e.tensor_reduce(
                lnt[:, mslot:mslot + 1], vsq[:, mslot * KC:(mslot + 1) * KC], AX.X, OP.add),
              reads=[rvs], writes=[r_lns], inc=dv)
            E("vector", lambda e, mslot=mslot: e.tensor_mul(
                lnt[:, 2 + mslot:3 + mslot], lnm[:, mslot:mslot + 1], lnm[:, mslot:mslot + 1]),
              reads=[r_lns], writes=[r_lns], inc=dv)
            E("vector", lambda e, mslot=mslot: e.tensor_scalar(
                lnt[:, mslot:mslot + 1], lnt[:, mslot:mslot + 1], 1.0 / D,
                lnt[:, 2 + mslot:3 + mslot], OP.mult, OP.subtract),
              reads=[r_lns], writes=[r_lns], inc=dv)
            E("vector", lambda e, mslot=mslot: e.tensor_scalar(
                lnt[:, mslot:mslot + 1], lnt[:, mslot:mslot + 1], 1e-5, None, OP.add),
              reads=[r_lns], writes=[r_lns], inc=dv)
            E("scalar", lambda e, mslot=mslot: e.activation(
                lnt[:, 4 + mslot:5 + mslot], lnt[:, mslot:mslot + 1], AF.Ln),
              reads=[r_lns], writes=[r_lns], inc=ac)
            E("vector", lambda e, mslot=mslot: e.tensor_scalar(
                lnt[:, 4 + mslot:5 + mslot], lnt[:, 4 + mslot:5 + mslot], -0.5, None, OP.mult),
              reads=[r_lns], writes=[r_lns], inc=dv)
            E("scalar", lambda e, mslot=mslot: e.activation(
                lnr[:, mslot:mslot + 1], lnt[:, 4 + mslot:5 + mslot], AF.Exp),
              reads=[r_lns], writes=[r_lns], inc=ac)
            E("vector", lambda e, mslot=mslot, vt=vt: e.tensor_scalar(
                scrD[:, :], vt[:, :], lnm[:, mslot:mslot + 1], lnr[:, mslot:mslot + 1],
                OP.subtract, OP.mult),
              reads=[rvt, r_lns], writes=[r_scrD], inc=dv)
            x1sl = (g * NM + m) % 2
            dma("gpsimd", qx1w[x1sl],
                bass.AP(x1_h, (r0 + m * 128) * D, [[D, 128], [1, D]]), scrD[:, :],
                reads=[r_scrD], writes=[r_x1h[x1sl]])
            for ccq in range(KC):
                sl = ccq % 2
                emit_qd(scrD[:, ccq * 128:(ccq + 1) * 128], r_scrD, 4 + sl,
                        qdt[:, sl * 128:(sl + 1) * 128], r_qd[sl])
                emit_transpose(qdt[:, sl * 128:(sl + 1) * 128], r_qd[sl],
                               Xd[:, ccq * GR + m * 128:ccq * GR + (m + 1) * 128], r_Xd,
                               extra=(fence() if (m == 0 and ccq == 0) else ()))

    def ffn1_group(g):
        half = g % 2
        rdd = r_Dd[half]
        dd = DdA if half == 0 else DdB
        esb_guard = list(pair_guard[0]) if g % NPAIR == 0 else []
        for it in range(NIT):
            sl = it % 2
            need_ib = (it + 1) * IT // 128
            while wq_pos[0] < min(need_ib * (D // WQK0), len(wq_units)):
                wq_step()
            ready = [r for ib in range(it * IT // 128, (it + 1) * IT // 128)
                     for ks in range(D // WQK0) for r in (w0rdy[ib][ks] or [])]
            dma("sync", qw0[sl],
                W0b[:, sl * KC * IT:(sl + 1) * KC * IT].rearrange("p (c n) -> p c n", c=KC),
                bass.AP(w0dq, it * IT, [[I, 128], [128 * I, KC], [1, IT]]),
                writes=[r_W0[sl]], extra=(ready + esb_guard))
            dma("sync", qb0[sl], b0b[0:1, sl * IT:(sl + 1) * IT],
                bass.AP(b0h, it * IT, [[IT, 1], [1, IT]]), writes=[r_b0[sl]])
            for m in range(NM):
                bank = 2 * (it % 2) + m
                rb = r_ps[bank]
                E("tensor", lambda e, sl=sl, bank=bank: e.matmul(
                    psb[bank][:, 0:IT], onesb[0:1, :], b0b[0:1, sl * IT:(sl + 1) * IT],
                    start=True, stop=False),
                  reads=[r_b0[sl], r_const], writes=[rb],
                  extra=(fence() if it == 0 and m == 0 else ()))
                for cc in range(KC):
                    E("tensor", lambda e, sl=sl, bank=bank, cc=cc, m=m: e.matmul(
                        psb[bank][:, 0:IT],
                        Xd[:, cc * GR + m * 128:cc * GR + (m + 1) * 128],
                        W0b[:, sl * KC * IT + cc * IT:sl * KC * IT + (cc + 1) * IT],
                        start=False, stop=(cc == KC - 1)),
                      reads=[r_Xd, r_W0[sl], r_b0[sl]], writes=[rb],
                      inc=(pe if cc == KC - 1 else None))
                dsl = (it * NM + m) % 2
                E("scalar", lambda e, bank=bank, dsl=dsl: e.activation(
                    dgt[:, dsl * IT:(dsl + 1) * IT], psb[bank][:, 0:IT], AF.Gelu),
                  reads=[rb], writes=[r_dg[dsl]], inc=ac)
                for tc in range(ITC):
                    ic = it * ITC + tc
                    sl2 = (ic + m) % 2
                    emit_qd(dgt[:, dsl * IT + tc * 128:dsl * IT + (tc + 1) * 128],
                            r_dg[dsl], 6 + sl2, qdt[:, sl2 * 128:(sl2 + 1) * 128], r_qd[sl2])
                    emit_transpose(qdt[:, sl2 * 128:(sl2 + 1) * 128], r_qd[sl2],
                                   dd[:, ic * GR + m * 128:ic * GR + (m + 1) * 128], rdd)

    def ffn2_pair(pair):
        while wq_pos[0] < len(wq_units):
            wq_step()
        wq_done = ([(dv, dv.v), (po, po.v), (ac, ac.v)]
                   + [(q, q.v) for q in qqs] + [(q, q.v) for q in qqi])
        first_w1 = [True]
        g0 = pair * NPAIR
        NHP = (NHQ + 1) // 2
        for hp in range(NHP):
            hqs = list(range(2 * hp, min(2 * hp + 2, NHQ)))
            nh = len(hqs)
            for hi in range(nh):
                dma("sync", qb1, b1b[0:1, :],
                    bass.AP(b1h, hqs[hi] * HQ, [[HQ, 1], [1, HQ]]),
                    writes=[r_b1])
                for mm in range(NMM):
                    bank = mm + 4 * hi
                    E("tensor", lambda e, bank=bank: e.matmul(
                        psb[bank][:, 0:HQ], onesb[0:1, :], b1b[0:1, :],
                        start=True, stop=False),
                      reads=[r_b1, r_const], writes=[r_ps[bank]],
                      inc=(pe if mm == NMM - 1 else None),
                      extra=(fence() if hp == 0 and mm == 0 and hi == 0 else ()))
            for blk in range(IC // W1B):
                wsl = blk % 2
                ready = []
                for hq in hqs:
                    for j in range(HQ // 128):
                        hb = (hq * HQ) // 128 + j
                        for kk in range(W1B):
                            ic = blk * W1B + kk
                            ready.extend(w1rdy[hb][ic * 128 // WQK1] or [])
                dma("sync", qw1[wsl],
                    W1b[:, wsl * W1B * nh * HQ:wsl * W1B * nh * HQ + W1B * nh * HQ]
                    .rearrange("p (c n) -> p c n", c=W1B),
                    bass.AP(w1dq, blk * W1B * 128 * D + hqs[0] * HQ,
                            [[D, 128], [128 * D, W1B], [1, nh * HQ]]),
                    writes=[r_W1[wsl]],
                    extra=(ready + (wq_done if first_w1[0] else [])))
                first_w1[0] = False
                for kk in range(W1B):
                    ic = blk * W1B + kk
                    for mm in range(NMM):
                        gg = mm // NM
                        m = mm % NM
                        ddh = (g0 + gg) % 2
                        ddt = DdA if ddh == 0 else DdB
                        for hi in range(nh):
                            bank = mm + 4 * hi
                            last = (blk == IC // W1B - 1 and kk == W1B - 1)
                            blk_end = (kk == W1B - 1 and mm == NMM - 1 and hi == nh - 1)
                            E("tensor", lambda e, ddt=ddt, ic=ic, m=m, wsl=wsl, kk=kk, hi=hi, nh=nh, bank=bank, last=last: e.matmul(
                                psb[bank][:, 0:HQ],
                                ddt[:, ic * GR + m * 128:ic * GR + (m + 1) * 128],
                                W1b[:, wsl * W1B * nh * HQ + (kk * nh + hi) * HQ:
                                     wsl * W1B * nh * HQ + (kk * nh + hi + 1) * HQ],
                                start=False, stop=last),
                              reads=[r_Dd[ddh], r_W1[wsl]],
                              writes=[r_ps[bank]],
                              inc=(pe if (last or blk_end) else None))
            for mm in range(NMM):
                for hi, hq in enumerate(hqs):
                    bank = mm + 4 * hi
                    if (mm + hi) % 2 == 0:
                        E("scalar", lambda e, mm=mm, bank=bank, hq=hq: e.activation(
                            Esb[:, mm * D + hq * HQ:mm * D + (hq + 1) * HQ],
                            psb[bank][:, 0:HQ], AF.Copy),
                          reads=[r_ps[bank]], writes=[r_Esb[mm]], inc=ac)
                    else:
                        E("vector", lambda e, mm=mm, bank=bank, hq=hq: e.tensor_copy(
                            Esb[:, mm * D + hq * HQ:mm * D + (hq + 1) * HQ],
                            psb[bank][:, 0:HQ]),
                          reads=[r_ps[bank]], writes=[r_Esb[mm]], inc=dv)

    def ln2_pair(pair):
        g0 = pair * NPAIR
        r0 = g0 * GR
        x1done = [(qx1w[0], qx1w[0].v), (qx1w[1], qx1w[1].v)]
        for mm in range(NMM):
            rows = r0 + mm * 128
            dma("sync", qx1r, X1rb[:, :], bass.AP(x1_h, rows * D, [[D, 128], [1, D]]),
                reads=[r_x1h[0], r_x1h[1]], writes=[r_x1rb], extra=x1done)
            E("vector", lambda e, mm=mm: e.scalar_tensor_tensor(
                v2t[:, :], Esb[:, mm * D:(mm + 1) * D], 1.0, X1rb[:, :],
                OP.mult, OP.add, accum_out=l2s[:, mm:mm + 1]),
              reads=[r_Esb[mm], r_x1rb], writes=[r_Xd, r_l2], inc=dv)
            for h8 in range(D // 512):
                E("scalar", lambda e, mm=mm, h8=h8: e.activation(
                    psb[mm][:, 0:512], v2t[:, h8 * 512:(h8 + 1) * 512], AF.Square,
                    accum_out=l2q[:, mm * 8 + h8:mm * 8 + h8 + 1]),
                  reads=[r_Xd], writes=[r_ps[mm], r_l2], inc=ac)
            E("vector", lambda e, mm=mm: e.tensor_reduce(
                l2m[:, mm:mm + 1], l2q[:, mm * 8:mm * 8 + D // 512], AX.X, OP.add),
              reads=[r_l2], writes=[r_l2], inc=dv)
            E("vector", lambda e, mm=mm: e.tensor_scalar(
                l2s[:, mm:mm + 1], l2s[:, mm:mm + 1], 1.0 / D, None, OP.mult),
              reads=[r_l2], writes=[r_l2], inc=dv)
            E("vector", lambda e, mm=mm: e.tensor_mul(
                l2r[:, mm:mm + 1], l2s[:, mm:mm + 1], l2s[:, mm:mm + 1]),
              reads=[r_l2], writes=[r_l2], inc=dv)
            E("vector", lambda e, mm=mm: e.tensor_scalar(
                l2m[:, mm:mm + 1], l2m[:, mm:mm + 1], 1.0 / D, l2r[:, mm:mm + 1],
                OP.mult, OP.subtract),
              reads=[r_l2], writes=[r_l2], inc=dv)
            E("vector", lambda e, mm=mm: e.tensor_scalar(
                l2m[:, mm:mm + 1], l2m[:, mm:mm + 1], 1e-5, None, OP.add),
              reads=[r_l2], writes=[r_l2], inc=dv)
            E("scalar", lambda e, mm=mm: e.activation(
                l2r[:, mm:mm + 1], l2m[:, mm:mm + 1], AF.Ln),
              reads=[r_l2], writes=[r_l2], inc=ac)
            E("vector", lambda e, mm=mm: e.tensor_scalar(
                l2r[:, mm:mm + 1], l2r[:, mm:mm + 1], -0.5, None, OP.mult),
              reads=[r_l2], writes=[r_l2], inc=dv)
            E("scalar", lambda e, mm=mm: e.activation(
                l2r[:, mm:mm + 1], l2r[:, mm:mm + 1], AF.Exp),
              reads=[r_l2], writes=[r_l2], inc=ac)
            E("vector", lambda e, mm=mm: e.tensor_scalar(
                v2t[:, :], v2t[:, :], l2s[:, mm:mm + 1], l2r[:, mm:mm + 1],
                OP.subtract, OP.mult),
              reads=[r_l2, r_Xd], writes=[r_Xd], inc=dv)
            dma("gpsimd", qout[mm % 2], bass.AP(out_h, rows * D, [[D, 128], [1, D]]),
                v2t[:, :], reads=[r_Xd], writes=[r_outh[mm % 2]])

    pair_guard = [[]]
    for pair in range(NG // NPAIR):
        for gg in range(NPAIR):
            g = pair * NPAIR + gg
            conv_group(g)
            ffn1_group(g)
        pump(8)
        ffn2_pair(pair)
        ln2_pair(pair)
        pair_guard[0] = [(dv, dv.v), (ac, ac.v)]

    E("gpsimd", lambda e: e.nop(),
      extra=[(qout[0], qout[0].v), (qout[1], qout[1].v)], inc=po)

    import contextlib
    stack = contextlib.ExitStack()
    for c in ctrs:
        c.sem = stack.enter_context(nc.semaphore(c.name))
    with nc.Block() as block:
        @block.sync
        def _(sync):
            plan.replay("sync", sync)

        @block.gpsimd
        def _(gpsimd):
            plan.replay("gpsimd", gpsimd)

        @block.tensor
        def _(tensor):
            plan.replay("tensor", tensor)

        @block.vector
        def _(vector):
            plan.replay("vector", vector)

        @block.scalar
        def _(scalar):
            plan.replay("scalar", scalar)
    stack.close()
    return nc


def make_core_inputs(ins, rows_slice, cfg):
    r0, r1 = rows_slice
    D = cfg["d"]
    u = ins["L0_attn_u"].reshape(-1, D)
    lam = ins["L0_attn_Lambda_ri"]
    pv = ins["L0_attn_p_ri"]
    qv = ins["L0_attn_q_ri"]
    vc = ins["L0_attn_Vc_ri"]
    ct = ins["L0_attn_Ct_ri"].reshape(-1, 2)
    return {
        "u_tok": np.ascontiguousarray(u[r0:r1]),
        "u_fm": np.ascontiguousarray(u[r0:r1].T),
        "w0n": np.ascontiguousarray(ins["L0_ffn_W0"]),
        "w1n": np.ascontiguousarray(ins["L0_ffn_W1"]),
        "b0h": np.ascontiguousarray(ins["L0_ffn_bias0"].reshape(1, -1)),
        "b1h": np.ascontiguousarray(ins["L0_ffn_bias1"].reshape(1, -1)),
        "lam_re": np.ascontiguousarray(lam[:, 0:1]),
        "lam_im": np.ascontiguousarray(lam[:, 1:2]),
        "pv_re": np.ascontiguousarray(pv[:, 0:1]),
        "pv_im": np.ascontiguousarray(pv[:, 1:2]),
        "qv_re": np.ascontiguousarray(qv[:, 0:1]),
        "qv_im": np.ascontiguousarray(qv[:, 1:2]),
        "ct_re": np.ascontiguousarray(ct[:, 0:1]),
        "ct_im": np.ascontiguousarray(ct[:, 1:2]),
        "vct_re": np.ascontiguousarray(vc[:, :, 0].T),
        "vct_im": np.ascontiguousarray(vc[:, :, 1].T),
        "bp": np.ascontiguousarray(ins["L0_attn_Bp"]),
        "dcoef": np.ascontiguousarray(ins["L0_attn_D"].reshape(1, 1)),
        "logstep": np.ascontiguousarray(ins["L0_attn_log_step"].reshape(1, 1)),
    }


CFG = dict(rows=1024, d=4096, ffn_i=14336, group_rows=256, pair_groups=2,
           i_tile=256, w1_block=4)
N_CORES = 8


def kernel(**inputs):
    from concourse.bass_utils import run_bass_kernel_spmd
    rows_total = inputs["L0_attn_u"].shape[0] * inputs["L0_attn_u"].shape[1]
    rpc = rows_total // N_CORES
    nc = build(CFG)
    in_maps = [make_core_inputs(inputs, (c * rpc, (c + 1) * rpc), CFG)
               for c in range(N_CORES)]
    res = run_bass_kernel_spmd(nc, in_maps, core_ids=list(range(N_CORES)))
    out = np.concatenate([np.asarray(r["out"]) for r in res.results], axis=0)
    return out.reshape(inputs["L0_attn_u"].shape).astype(np.float32)


# version 17 (final)
# speedup vs baseline: 1.0996x; 1.0996x over previous
import numpy as np
import ml_dtypes
import concourse.bass as bass
import concourse.mybir as mybir

F32 = mybir.dt.float32
BF16 = mybir.dt.bfloat16
F8 = mybir.dt.float8e4
AF = mybir.ActivationFunctionType
OP = mybir.AluOpType
AX = mybir.AxisListType
bfd = ml_dtypes.bfloat16


class Ctr:
    __slots__ = ("name", "sem", "v", "kind")

    def __init__(self, name, kind):
        self.name, self.kind, self.sem, self.v = name, kind, None, 0


class Res:
    __slots__ = ("name", "writer", "readers")

    def __init__(self, name=""):
        self.name, self.writer, self.readers = name, None, []


class Plan:
    ENGINES = ("sync", "gpsimd", "tensor", "vector", "scalar")

    def __init__(self):
        self.ops = {e: [] for e in self.ENGINES}

    def emit(self, eng, fn, reads=(), writes=(), inc=None, extra=()):
        waits = {}

        def add(cv):
            if cv is not None and waits.get(cv[0], -1) < cv[1]:
                waits[cv[0]] = cv[1]

        for r in reads:
            add(r.writer)
        for r in writes:
            add(r.writer)
            for cv in r.readers:
                add(cv)
        if inc is not None and inc.kind == "d":
            waits.pop(inc, None)
        for cv in extra:
            add(cv)
        rec = None
        if inc is not None:
            inc.v += 16 if inc.kind == "d" else 1
            rec = (inc, inc.v)
        self.ops[eng].append((list(waits.items()), fn, rec))
        if rec is not None:
            for r in writes:
                r.writer, r.readers = rec, []
            for r in reads:
                r.readers.append(rec)
        return rec

    def replay(self, eng_name, eng):
        done = {}
        for waits, fn, rec in self.ops[eng_name]:
            for c, v in waits:
                if done.get(c, 0) < v:
                    eng.wait_ge(c.sem, v)
                    done[c] = v
            inst = fn(eng)
            if rec is not None:
                c, _ = rec
                inst.then_inc(c.sem, 16 if c.kind == "d" else 1)


def host_constants(L):
    l = np.arange(L)
    x = (l / L).astype(np.float32)
    ang = (-2.0 * np.pi * x.astype(np.float64)).astype(np.float32)
    om = (np.cos(ang.astype(np.float64)).astype(np.float32)
          + 1j * np.sin(ang.astype(np.float64)).astype(np.float32)).astype(np.complex64)
    one = np.complex64(1.0)
    ratio = ((one - om) / (one + om)).astype(np.complex64)
    T = ratio.imag.astype(np.float32)
    c = (np.complex64(2.0) / (one + om)).astype(np.complex64)
    KC = L // 128
    cre = np.ascontiguousarray(c.real.reshape(KC, 128).T).astype(np.float32)
    cim = np.ascontiguousarray(c.imag.reshape(KC, 128).T).astype(np.float32)
    p = np.arange(128)
    tf = np.arange(128)
    A = 2.0 * np.pi * np.outer(p, tf) / 128.0
    cosA = np.cos(A).astype(np.float32)
    sinA = np.sin(A).astype(np.float32)
    lcol = l.reshape(KC, 128).T
    tc = np.arange(KC)
    B = 2.0 * np.pi * lcol[:, :, None].astype(np.float64) * tc[None, None, :] / L
    cB = (np.cos(B) / L).astype(np.float32).reshape(128, KC * KC)
    sB = (np.sin(B) / L).astype(np.float32).reshape(128, KC * KC)
    return dict(T=T, cre=cre, cim=cim, cosA=cosA, sinA=sinA, cB=cB, sB=sB)


def build(cfg):
    ROWS, D, I = cfg["rows"], cfg["d"], cfg["ffn_i"]
    L, KC, IC = D, D // 128, I // 128
    GR = cfg["group_rows"]
    NG, NM = ROWS // GR, GR // 128
    NPAIR = cfg.get("pair_groups", 2)
    assert NG % NPAIR == 0
    IT = cfg["i_tile"]
    NIT, ITC = I // IT, IT // 128
    HQ = min(512, D)
    NHQ = D // HQ
    W1B = cfg.get("w1_block", 8)
    assert IC % W1B == 0
    NMM = NPAIR * NM
    assert NMM <= 4

    nc = bass.Bass("TRN2", target_bir_lowering=False)

    u_tok = nc.dram_tensor("u_tok", [ROWS, D], F32, kind="ExternalInput")
    u_fm = nc.dram_tensor("u_fm", [D, ROWS], F32, kind="ExternalInput")
    w0n = nc.dram_tensor("w0n", [I, D], BF16, kind="ExternalInput")
    w1n = nc.dram_tensor("w1n", [D, I], BF16, kind="ExternalInput")
    w0dq = nc.dram_tensor("w0dq", [D, I], BF16)
    w1dq = nc.dram_tensor("w1dq", [I, D], BF16)
    b0h = nc.dram_tensor("b0h", [1, I], BF16, kind="ExternalInput")
    b1h = nc.dram_tensor("b1h", [1, D], BF16, kind="ExternalInput")
    sm = {}
    for nm in ("lam_re", "lam_im", "pv_re", "pv_im", "qv_re", "qv_im",
               "ct_re", "ct_im", "bp"):
        sm[nm] = nc.dram_tensor(nm, [64, 1], F32, kind="ExternalInput")
    sm["vct_re"] = nc.dram_tensor("vct_re", [64, 64], F32, kind="ExternalInput")
    sm["vct_im"] = nc.dram_tensor("vct_im", [64, 64], F32, kind="ExternalInput")
    sm["dcoef"] = nc.dram_tensor("dcoef", [1, 1], F32, kind="ExternalInput")
    sm["logstep"] = nc.dram_tensor("logstep", [1, 1], F32, kind="ExternalInput")
    out_h = nc.dram_tensor("out", [ROWS, D], F32, kind="ExternalOutput")
    x1_h = nc.dram_tensor("x1sc", [ROWS, D], BF16)
    kpad_h = nc.dram_tensor("kpad", [1, 127 + L], BF16)

    C = host_constants(L)
    con = {}
    for nm, arr in (("ident", np.eye(128, dtype=bfd)),
                    ("tb", np.broadcast_to(C["T"], (64, L)).copy()),
                    ("cre", C["cre"]), ("cim", C["cim"]),
                    ("cosA", C["cosA"]), ("sinA", C["sinA"]),
                    ("cB", C["cB"]), ("sB", C["sB"]),
                    ("ones_b", np.ones((1, 128), dtype=bfd)),
                    ("ones_f", np.ones((1, 128), dtype=np.float32))):
        con[nm] = nc.inline_tensor(np.ascontiguousarray(arr), name=nm + "_c")

    base = ((nc.sbuf_base + 31) // 32) * 32
    ptr = [base]
    TENS = {}

    def sal(name, shape, dtype, at=None):
        per = int(np.prod(shape[1:])) * mybir.dt.size(dtype)
        if at is None:
            at = ptr[0]
            ptr[0] = ((at + per + 31) // 32) * 32
            assert ptr[0] <= nc.sbuf_top, f"SBUF overflow at {name}: {ptr[0]} > {nc.sbuf_top}"
        h = nc.alloc_sbuf_tensor_at(name, shape, dtype, offset=at)
        TENS[name] = (h, at, per)
        return h

    Mt = sal("Mt", [128, L], BF16)
    Xd = sal("Xd", [128, KC * GR], BF16)
    XdB = sal("XdB", [128, KC * GR], BF16)
    XdH = [Xd, XdB]
    W0b = sal("W0b", [128, 2 * KC * IT], BF16)
    NH2 = 2 if NHQ >= 2 else 1
    W1b = sal("W1b", [128, 2 * W1B * NH2 * HQ], BF16)
    X1rb = sal("X1rb", [128, D], BF16)
    b0b = sal("b0b", [1, 2 * IT], BF16)
    b1b = sal("b1b", [1, HQ], BF16)
    ident = sal("ident", [128, 128], BF16)
    onesb = sal("onesb", [1, 128], BF16)
    onesf = sal("onesf", [1, 128], F32)
    zpref = sal("zpref", [1, 128], BF16)
    sv = {}
    for nm in ("lam_re", "lam_im", "pv_re", "pv_im", "qv_re", "qv_im", "ct_re",
               "ct_im", "bp", "bre", "bim", "neglamre", "lamre2", "s64"):
        sv[nm] = sal("sv_" + nm, [64, 1], F32)
    opd = sal("opd", [128, 1], F32)
    V1 = sal("V1", [64, 8], F32)
    V2 = sal("V2", [64, 8], F32)
    sc1 = sal("sc1", [1, 1], F32)
    sc2 = sal("sc2", [1, 1], F32)
    vst = sal("vst", [64, 2], F32)
    vsums = sal("vsums", [128, 4 * KC], F32)
    vsq = sal("vsq", [128, 4 * KC], F32)
    sqscr = sal("sqscr", [128, 128], BF16)
    lnm = sal("lnm", [128, 4], F32)
    lnr = sal("lnr", [128, 4], F32)
    lnt = sal("lnt", [128, 8], F32)
    qa = sal("qa", [128, 24], F32)
    qs = sal("qs", [128, 24], F32)
    qr = sal("qr", [128, 24], F32)
    l2s = sal("l2s", [128, 4], F32)
    l2q = sal("l2q", [128, 8 * 4], F32)
    l2m = sal("l2m", [128, 4], F32)
    l2r = sal("l2r", [128, 4], F32)
    q8t = sal("q8t", [128, 8 * 128], F8)
    xqd = sal("xqd", [128, 2 * 8 * 128], BF16)
    dgt = sal("dgt", [128, 2 * IT], BF16)
    DdA = sal("DdA", [128, IC * GR], BF16)
    DdB = sal("DdB", [128, IC * GR], BF16)
    atA, atB = TENS["DdA"][1], TENS["DdB"][1]
    arena_bytes = 2 * IC * GR * 2

    Esb = sal("Esb", [128, NMM * D], BF16,
              at=(TENS["W0b"][1] if NMM * D * 2 <= TENS["W0b"][2] else None))
    WQK0 = min(1024, D)
    WQK1 = min(1024, I)
    WQS = max(WQK0, WQK1)
    _need = WQS * 2 + WQS + WQS * 2 + 2 * 2048 + 512
    _span = TENS["W1b"][2]
    if TENS["X1rb"][1] == TENS["W1b"][1] + TENS["W1b"][2]:
        _span += TENS["X1rb"][2]
    _fit = _need <= _span
    _o = [TENS["W1b"][1] if _fit else None]
    qesc = []

    def _qal(name, shape, dtype):
        n = int(np.prod(shape[1:])) * mybir.dt.size(dtype)
        at = _o[0]
        h = sal(name, shape, dtype, at=at)
        if at is not None:
            _o[0] = at + n
        return h

    wq_in = _qal("wq_in", [128, WQS], BF16)
    wq_q8 = _qal("wq_q8", [128, WQS], F8)
    wq_out = _qal("wq_out", [128, WQS], BF16)
    wq_stg = _qal("wq_stg", [128, 2 * 1024], BF16)
    wq_am = _qal("wq_am", [128, 16], F32)
    wq_s = _qal("wq_s", [128, 16], F32)
    wq_r = _qal("wq_r", [128, 16], F32)
    v2t = sal("v2t", [128, D], F32,
              at=(TENS["Xd"][1] if D * 4 <= TENS["Xd"][2] else None))

    scr_fit = (KC * GR * 2 + 2 * D * 4 + 4 * 128 * 4) <= IC * GR * 2
    conv_scr = []
    for h, at0 in (("A", atA), ("B", atB)):
        o = at0 if scr_fit else None
        def nxt(o, n):
            return (None, None) if o is None else (o, o + n)
        a, o = nxt(o, KC * GR * 2)
        ut = sal("Ut" + h, [128, KC * GR], BF16, at=a)
        a, o = nxt(o, D * 4)
        vt0 = sal("vt0" + h, [128, D], F32, at=a)
        a, o = nxt(o, D * 4)
        vt1 = sal("vt1" + h, [128, D], F32, at=a)
        a, o = nxt(o, 8 * 128 * 4)
        uj = sal("uj" + h, [128, 8 * 128], F32, at=a)
        conv_scr.append((ut, (vt0, vt1), uj))

    pho = [atA]
    PH0_BYTES = (5 * L * 4 + 5 * KC * KC * 4 + 2 * 128 * 4 + 2 * 64 * 4
                 + KC * 8 * 4 + KC * 10 * 4 + 4 * KC * 4 + KC * 2)

    def ph(name, shape, dtype):
        n = int(np.prod(shape[1:])) * mybir.dt.size(dtype)
        if PH0_BYTES <= arena_bytes:
            h = sal(name, shape, dtype, at=pho[0])
            pho[0] += n
        else:
            h = sal(name, shape, dtype)
        return h

    Tb = ph("Tb", [64, L], F32)
    dimt = ph("dimt", [64, L], F32)
    d2t = ph("d2t", [64, L], F32)
    rCt = ph("rCt", [128, KC * KC], F32)
    rSt = ph("rSt", [128, KC * KC], F32)
    rS2 = ph("rS2", [128, KC * KC], F32)
    cbt = ph("cbt", [128, KC * KC], F32)
    sbt = ph("sbt", [128, KC * KC], F32)
    cosA_s = ph("cosA_s", [128, 128], F32)
    sinA_s = ph("sinA_s", [128, 128], F32)
    cret = ph("cret", [128, KC], F32)
    cimt = ph("cimt", [128, KC], F32)
    Rt = ph("Rt", [128, KC], F32)
    It_ = ph("It_", [128, KC], F32)
    ktileb = ph("ktileb", [128, KC], BF16)
    vctre_s = ph("vctre_s", [64, 64], F32)
    vctim_s = ph("vctim_s", [64, 64], F32)
    kall = ph("kall", [128, KC * 8], F32)
    ars = ph("ars", [128, KC * 10], F32)
    dreinv = ph("dreinv", [64, L], F32)
    diminv = ph("diminv", [64, L], F32)

    psb = [nc.alloc_psum_tensor(f"psb{i}", [128, 512], F32) for i in range(8)]

    plan = Plan()
    ctrs = []

    def CT(name, kind):
        c = Ctr(name, kind)
        ctrs.append(c)
        return c

    pe, dv, ac, po = CT("pe", "c"), CT("dv", "c"), CT("ac", "c"), CT("po", "c")
    qc = CT("qc", "d")
    qm = CT("qm", "d")
    qsm = CT("qsm", "d")
    qkp = CT("qkp", "d")
    qw0 = [CT("qw0a", "d"), CT("qw0b", "d")]
    qw1 = [CT("qw1a", "d"), CT("qw1b", "d")]
    qb0 = [CT("qb0a", "d"), CT("qb0b", "d")]
    qb1 = CT("qb1", "d")
    qut = [[CT(f"qut{h}{i}", "d") for i in range(4)] for h in range(2)]
    quj = [CT(f"quj{i}", "d") for i in range(8)]
    qx1w = [CT("qx1wa", "d"), CT("qx1wb", "d")]
    qx1r = CT("qx1r", "d")
    qout = [CT("qouta", "d"), CT("qoutb", "d")]
    qqi = [CT("qqia", "d"), CT("qqib", "d")]
    qqs = [CT(f"qqs{i}", "d") for i in range(4)]

    r_const = Res("const")
    r_small = Res("small")
    r_M = Res("M")
    r_kpad = Res("kpad")
    r_ps = [Res(f"ps{i}") for i in range(8)]
    r_pst = [Res(f"pst{i}") for i in range(8)]
    r_W0 = [Res("W0a"), Res("W0b")]
    r_W1 = [Res("W1a"), Res("W1b")]
    r_b0 = [Res("b0a"), Res("b0b")]
    r_b1 = Res("b1")
    r_Ut4 = [[Res(f"Ut{h}{i}") for i in range(4)] for h in range(2)]
    r_vt = [[Res(f"vt{h}{s}") for s in range(2)] for h in range(2)]
    r_uj = [Res(f"uj{i}") for i in range(8)]
    r_Xd = [[Res(f"Xd{h}_{i}") for i in range(KC)] for h in range(2)]
    r_Dd = [Res("DdA"), Res("DdB")]
    r_ph0 = Res("ph0")
    r_kall = Res("kall")
    r_ar = Res("ar")
    r_kt = Res("ktile")
    r_zp = Res("zpref")
    r_x1h = [Res("x1hbm0"), Res("x1hbm1")]
    r_x1rb = Res("x1rb")
    r_q8 = Res("q8")
    r_xq = [Res(f"xq_{i}") for i in range(2)]
    r_dg = [Res(f"dg_{i}") for i in range(2)]
    r_Esb = [Res(f"E{m}") for m in range(NMM)]
    r_lns = [Res(f"lns{i}") for i in range(4)]
    r_qsc = [Res(f"qsc{i}") for i in range(4)]
    r_vsums = [Res(f"vs{i}") for i in range(4)]
    r_sqscr = Res("sqscr")
    r_outh = [Res("outh0"), Res("outh1")]
    r_wqin = Res("wqin")
    r_wq8 = Res("wq8")
    r_wqout = Res("wqout")
    r_wqsc = Res("wqsc")
    r_wqstg = [Res("wqstg0"), Res("wqstg1")]
    r_psq = [Res(f"psq{i}") for i in range(8)]
    r_opd = Res("opd")
    r_sc = Res("sc")
    r_l2 = Res("l2")

    E = plan.emit

    def dma(engq, q, outap, inap, reads=(), writes=(), extra=()):
        return E(engq, lambda e: e.dma_start(outap, inap),
                 reads=reads, writes=writes, inc=q, extra=extra)

    def fence():
        return [(dv, dv.v), (ac, ac.v)]

    wq_units = []
    for ib in range(I // 128):
        for ks in range(D // WQK0):
            wq_units.append(("w0", ib, ks))
    for hb in range(D // 128):
        for cs in range(I // WQK1):
            wq_units.append(("w1", hb, cs))
    w0rdy = [[None] * (D // WQK0) for _ in range(I // 128)]
    w1rdy = [[None] * (I // WQK1) for _ in range(D // 128)]
    wq_pos = [0]
    wq_store_ct = [0]
    wq_evac_ct = [0]

    def wq_step():
        if wq_pos[0] >= len(wq_units):
            return
        kind, blk, span = wq_units[wq_pos[0]]
        wq_pos[0] += 1
        W = WQK0 if kind == "w0" else WQK1
        NCH = W // 128
        src_h = w0n if kind == "w0" else w1n
        dst_h = w0dq if kind == "w0" else w1dq
        rowlen = D if kind == "w0" else I
        outrow = I if kind == "w0" else D
        off = blk * 128 * rowlen + span * W
        inap = bass.AP(src_h, off, [[rowlen, 128], [1, W]])
        int_ = wq_in[:, 0:W]
        am_, s_, r_ = wq_am[:, 0:NCH], wq_s[:, 0:NCH], wq_r[:, 0:NCH]
        dma("sync", qqi[wq_pos[0] % 2], int_, inap, writes=[r_wqin])
        in3 = bass.AP(wq_in, 0, [[WQS, 128], [128, NCH], [1, 128]])
        E("vector", lambda e: e.tensor_reduce(am_, in3, AX.X, OP.max,
                                              apply_absolute_value=True),
          reads=[r_wqin], writes=[r_wqsc], inc=dv)
        E("vector", lambda e: e.tensor_scalar(s_, am_, 1e-4, 1.0 / 224.0, OP.max, OP.mult),
          reads=[r_wqsc], writes=[r_wqsc], inc=dv)
        E("vector", lambda e: e.reciprocal(r_, s_), reads=[r_wqsc], writes=[r_wqsc], inc=dv)
        rb3 = bass.AP(wq_r, 0, [[16, 128], [1, NCH], [0, 128]])
        sb3 = bass.AP(wq_s, 0, [[16, 128], [1, NCH], [0, 128]])
        q83 = bass.AP(wq_q8, 0, [[WQS, 128], [128, NCH], [1, 128]])
        o3 = bass.AP(wq_out, 0, [[WQS, 128], [128, NCH], [1, 128]])
        E("vector", lambda e: e.tensor_tensor(q83, in3, rb3, OP.mult),
          reads=[r_wqin, r_wqsc], writes=[r_wq8], inc=dv)
        E("vector", lambda e: e.tensor_tensor(o3, q83, sb3, OP.mult),
          reads=[r_wq8, r_wqsc], writes=[r_wqout], inc=dv)
        nbatch = (NCH + 7) // 8
        recs = []
        for b in range(nbatch):
            j0, j1 = b * 8, min(b * 8 + 8, NCH)
            ssl = wq_evac_ct[0] % 2
            wq_evac_ct[0] += 1
            for j in range(j0, j1):
                pv = psb[7][:, :].bitcast(BF16)[:, (j - j0) * 128:(j - j0 + 1) * 128]
                E("tensor", lambda e, j=j, pv=pv: e.transpose(
                    pv, wq_out[:, j * 128:(j + 1) * 128], ident[:, :]),
                  reads=[r_wqout, r_const], writes=[r_psq[j - j0]], inc=pe)
            nb = (j1 - j0) * 128
            pall = psb[7][:, :].bitcast(BF16)[:, 0:nb]
            if ssl == 0:
                E("vector", lambda e, pall=pall, ssl=ssl, nb=nb:
                  e.tensor_copy(wq_stg[:, ssl * 1024:ssl * 1024 + nb], pall),
                  reads=[r_psq[j] for j in range(j1 - j0)], writes=[r_wqstg[ssl]], inc=dv)
            else:
                E("scalar", lambda e, pall=pall, ssl=ssl, nb=nb:
                  e.activation(wq_stg[:, ssl * 1024:ssl * 1024 + nb], pall, AF.Copy),
                  reads=[r_psq[j] for j in range(j1 - j0)], writes=[r_wqstg[ssl]], inc=ac)
            ssl_s = wq_store_ct[0] % 4
            prev = wq_store_ct[0] - 4
            ex = [(qqs[ssl_s], 16 * (prev // 4 + 1))] if prev >= 0 else []
            wq_store_ct[0] += 1
            stoff = (span * W + j0 * 128) * outrow + blk * 128
            stap = bass.AP(dst_h, stoff,
                           [[outrow, 128], [128 * outrow, j1 - j0], [1, 128]])
            recs.append(dma("gpsimd", qqs[ssl_s], stap,
                            wq_stg[:, ssl * 1024:ssl * 1024 + nb], reads=[r_wqstg[ssl]],
                            extra=ex))
        if kind == "w0":
            w0rdy[blk][span] = recs
        else:
            w1rdy[blk][span] = recs

    def pump(n):
        for _ in range(n):
            wq_step()

    dma("sync", qc, ident[:, :], con["ident"].ap(), writes=[r_const])
    dma("sync", qc, onesb[:, :], con["ones_b"].ap(), writes=[r_const])
    dma("sync", qc, onesf[:, :], con["ones_f"].ap(), writes=[r_const])
    dma("sync", qc, Tb[:, :], con["tb"].ap(), writes=[r_ph0])
    dma("sync", qc, cret[:, :], con["cre"].ap(), writes=[r_const])
    dma("sync", qc, cimt[:, :], con["cim"].ap(), writes=[r_const])
    dma("sync", qc, cbt[:, :], con["cB"].ap(), writes=[r_ph0])
    dma("sync", qc, sbt[:, :], con["sB"].ap(), writes=[r_ph0])
    dma("sync", qc, cosA_s[:, :], con["cosA"].ap(), writes=[r_ph0])
    dma("sync", qc, sinA_s[:, :], con["sinA"].ap(), writes=[r_ph0])
    for nm in ("lam_re", "lam_im", "pv_re", "pv_im", "qv_re", "qv_im",
               "ct_re", "ct_im", "bp"):
        dma("gpsimd", qsm, sv[nm][:, :], sm[nm].ap(), writes=[r_small])
    dma("gpsimd", qsm, vctre_s[:, :], sm["vct_re"].ap(), writes=[r_small])
    dma("gpsimd", qsm, vctim_s[:, :], sm["vct_im"].ap(), writes=[r_small])
    dma("gpsimd", qsm, sc1[:, :], sm["logstep"].ap(), writes=[r_sc])
    dma("gpsimd", qsm, opd[0:1, 0:1], sm["dcoef"].ap(), writes=[r_opd])

    r_const.writer = (qc, qc.v)
    r_ph0.writer = (qc, qc.v)
    r_small.writer = (qsm, qsm.v)
    r_sc.writer = (qsm, qsm.v)
    r_opd.writer = (qsm, qsm.v)

    E("scalar", lambda e: e.activation(sc2[0:1, :], sc1[0:1, :], AF.Exp),
      reads=[r_sc], writes=[r_sc], inc=ac)
    E("vector", lambda e: e.reciprocal(sc1[0:1, :], sc2[0:1, :]),
      reads=[r_sc], writes=[r_sc], inc=dv)
    E("vector", lambda e: e.tensor_scalar(sc2[0:1, :], sc1[0:1, :], 2.0, None, OP.mult),
      reads=[r_sc], writes=[r_sc], inc=dv)
    E("tensor", lambda e: e.matmul(psb[6][0:64, 0:1], onesf[0:1, 0:64], sc2[0:1, :],
                                   start=True, stop=True),
      reads=[r_sc, r_const], writes=[r_ps[6]], inc=pe)
    E("vector", lambda e: e.tensor_copy(sv["s64"][:, :], psb[6][0:64, 0:1]),
      reads=[r_ps[6]], writes=[r_small], inc=dv)
    E("tensor", lambda e: e.matmul(psb[6][:, 8:9], onesf[0:1, 0:128], opd[0:1, 0:1],
                                   start=True, stop=True),
      reads=[r_opd, r_const], writes=[r_ps[6]], inc=pe)
    E("vector", lambda e: e.tensor_scalar(opd[:, :], psb[6][:, 8:9], 1.0, None, OP.add),
      reads=[r_ps[6]], writes=[r_opd], inc=dv)

    E("tensor", lambda e: e.matmul(psb[6][0:64, 0:1], vctre_s[:, :], sv["bp"][:, :],
                                   start=True, stop=True),
      reads=[r_small, r_ph0], writes=[r_ps[6]], inc=pe)
    E("vector", lambda e: e.tensor_copy(sv["bre"][:, :], psb[6][0:64, 0:1]),
      reads=[r_ps[6]], writes=[r_small], inc=dv)
    E("tensor", lambda e: e.matmul(psb[6][0:64, 0:1], vctim_s[:, :], sv["bp"][:, :],
                                   start=True, stop=True),
      reads=[r_small, r_ph0], writes=[r_ps[6]], inc=pe)
    E("vector", lambda e: e.tensor_copy(sv["bim"][:, :], psb[6][0:64, 0:1]),
      reads=[r_ps[6]], writes=[r_small], inc=dv)
    E("vector", lambda e: e.tensor_scalar(sv["neglamre"][:, :], sv["lam_re"][:, :], -1.0, None, OP.mult),
      reads=[r_small], writes=[r_small], inc=dv)
    E("vector", lambda e: e.tensor_mul(sv["lamre2"][:, :], sv["lam_re"][:, :], sv["lam_re"][:, :]),
      reads=[r_small], writes=[r_small], inc=dv)

    def vcomb(col, a1, b1_, a2, b2_, sgn, also_neg_to=None):
        E("vector", lambda e: e.tensor_mul(vst[:, 0:1], sv[a1][:, :], sv[b1_][:, :]),
          reads=[r_small], writes=[r_lns[0]], inc=dv)
        E("vector", lambda e: e.tensor_mul(vst[:, 1:2], sv[a2][:, :], sv[b2_][:, :]),
          reads=[r_small], writes=[r_lns[0]], inc=dv)
        op = OP.add if sgn > 0 else OP.subtract
        E("vector", lambda e: e.tensor_tensor(V1[:, col:col + 1], vst[:, 0:1], vst[:, 1:2], op),
          reads=[r_lns[0]], writes=[r_small], inc=dv)
        if also_neg_to is not None:
            E("vector", lambda e: e.tensor_scalar(V2[:, also_neg_to:also_neg_to + 1],
                                                  V1[:, col:col + 1], -1.0, None, OP.mult),
              reads=[r_small], writes=[r_small], inc=dv)

    def vcopy(dst_col, src_col):
        E("vector", lambda e: e.tensor_copy(V2[:, dst_col:dst_col + 1], V1[:, src_col:src_col + 1]),
          reads=[r_small], writes=[r_small], inc=dv)

    vcomb(0, "ct_re", "bre", "ct_im", "bim", +1, also_neg_to=1)
    vcomb(1, "ct_re", "bim", "ct_im", "bre", -1)
    vcopy(0, 1)
    vcomb(2, "ct_re", "pv_re", "ct_im", "pv_im", +1, also_neg_to=3)
    vcomb(3, "ct_re", "pv_im", "ct_im", "pv_re", -1)
    vcopy(2, 3)
    vcomb(4, "qv_re", "bre", "qv_im", "bim", +1, also_neg_to=5)
    vcomb(5, "qv_re", "bim", "qv_im", "bre", -1)
    vcopy(4, 5)
    vcomb(6, "qv_re", "pv_re", "qv_im", "pv_im", +1, also_neg_to=7)
    vcomb(7, "qv_re", "pv_im", "qv_im", "pv_re", -1)
    vcopy(6, 7)

    E("vector", lambda e: e.tensor_scalar(dimt[:, :], Tb[:, :], sv["s64"][:, :], None, OP.mult),
      reads=[r_ph0, r_small], writes=[r_ph0], inc=dv)
    E("vector", lambda e: e.tensor_scalar(dimt[:, :], dimt[:, :], sv["lam_im"][:, :], None, OP.subtract),
      reads=[r_small, r_ph0], writes=[r_ph0], inc=dv)
    E("vector", lambda e: e.tensor_mul(d2t[:, :], dimt[:, :], dimt[:, :]),
      reads=[r_ph0], writes=[r_ph0], inc=dv)
    E("vector", lambda e: e.tensor_scalar(d2t[:, :], d2t[:, :], sv["lamre2"][:, :], None, OP.add),
      reads=[r_small, r_ph0], writes=[r_ph0], inc=dv)
    E("vector", lambda e: e.reciprocal(d2t[:, :], d2t[:, :]),
      reads=[r_ph0], writes=[r_ph0], inc=dv)
    E("vector", lambda e: e.tensor_scalar(dreinv[:, :], d2t[:, :], sv["neglamre"][:, :], None, OP.mult),
      reads=[r_small, r_ph0], writes=[r_ph0], inc=dv)
    E("vector", lambda e: e.tensor_mul(diminv[:, :], dimt[:, :], d2t[:, :]),
      reads=[r_ph0], writes=[r_ph0], inc=dv)

    for cc in range(KC):
        E("tensor", lambda e, cc=cc: e.matmul(psb[6][:, 0:8],
                                              dreinv[:, cc * 128:(cc + 1) * 128],
                                              V1[:, :], start=True, stop=False),
          reads=[r_ph0, r_small], writes=[r_ps[6]])
        E("tensor", lambda e, cc=cc: e.matmul(psb[6][:, 0:8],
                                              diminv[:, cc * 128:(cc + 1) * 128],
                                              V2[:, :], start=False, stop=True),
          reads=[r_ph0, r_small], writes=[r_ps[6]], inc=pe)
        E("vector", lambda e, cc=cc: e.tensor_copy(kall[:, 8 * cc:8 * cc + 8], psb[6][:, 0:8]),
          reads=[r_ps[6]], writes=[r_kall], inc=dv)

    def kv(i):
        return bass.AP(kall, i, [[KC * 8, 128], [8, KC]])

    def scr(i):
        return ars[:, i * KC:(i + 1) * KC]

    def V(fn, reads, writes):
        E("vector", fn, reads=reads, writes=writes, inc=dv)

    V(lambda e: e.tensor_scalar(scr(0), kv(6), 1.0, None, OP.add), [r_kall], [r_ar])
    V(lambda e: e.tensor_mul(scr(1), scr(0), scr(0)), [r_ar], [r_ar])
    V(lambda e: e.tensor_mul(scr(2), kv(7), kv(7)), [r_kall], [r_ar])
    V(lambda e: e.tensor_add(scr(1), scr(1), scr(2)), [r_ar], [r_ar])
    V(lambda e: e.reciprocal(scr(1), scr(1)), [r_ar], [r_ar])
    V(lambda e: e.tensor_mul(scr(2), kv(2), kv(4)), [r_kall], [r_ar])
    V(lambda e: e.tensor_mul(scr(3), kv(3), kv(5)), [r_kall], [r_ar])
    V(lambda e: e.tensor_sub(scr(2), scr(2), scr(3)), [r_ar], [r_ar])
    V(lambda e: e.tensor_mul(scr(3), kv(2), kv(5)), [r_kall], [r_ar])
    V(lambda e: e.tensor_mul(scr(4), kv(3), kv(4)), [r_kall], [r_ar])
    V(lambda e: e.tensor_add(scr(3), scr(3), scr(4)), [r_ar], [r_ar])
    V(lambda e: e.tensor_mul(scr(4), scr(2), scr(0)), [r_ar], [r_ar])
    V(lambda e: e.tensor_mul(scr(5), scr(3), kv(7)), [r_ar, r_kall], [r_ar])
    V(lambda e: e.tensor_add(scr(4), scr(4), scr(5)), [r_ar], [r_ar])
    V(lambda e: e.tensor_mul(scr(4), scr(4), scr(1)), [r_ar], [r_ar])
    V(lambda e: e.tensor_mul(scr(5), scr(3), scr(0)), [r_ar], [r_ar])
    V(lambda e: e.tensor_mul(scr(6), scr(2), kv(7)), [r_ar, r_kall], [r_ar])
    V(lambda e: e.tensor_sub(scr(5), scr(5), scr(6)), [r_ar], [r_ar])
    V(lambda e: e.tensor_mul(scr(5), scr(5), scr(1)), [r_ar], [r_ar])
    V(lambda e: e.tensor_sub(scr(6), kv(0), scr(4)), [r_kall, r_ar], [r_ar])
    V(lambda e: e.tensor_sub(scr(7), kv(1), scr(5)), [r_kall, r_ar], [r_ar])
    V(lambda e: e.tensor_mul(scr(8), cret[:, :], scr(6)), [r_const, r_ar], [r_ar])
    V(lambda e: e.tensor_mul(scr(9), cimt[:, :], scr(7)), [r_const, r_ar], [r_ar])
    V(lambda e: e.tensor_sub(Rt[:, :], scr(8), scr(9)), [r_ar], [r_ar])
    V(lambda e: e.tensor_mul(scr(8), cret[:, :], scr(7)), [r_const, r_ar], [r_ar])
    V(lambda e: e.tensor_mul(scr(9), cimt[:, :], scr(6)), [r_const, r_ar], [r_ar])
    V(lambda e: e.tensor_add(It_[:, :], scr(8), scr(9)), [r_ar], [r_ar])

    Rexp = bass.AP(Rt, 0, [[KC, 128], [1, KC], [0, KC]])
    Iexp = bass.AP(It_, 0, [[KC, 128], [1, KC], [0, KC]])
    V(lambda e: e.tensor_mul(rCt[:, :], cbt[:, :], Rexp), [r_ph0, r_ar], [r_ph0])
    V(lambda e: e.tensor_mul(rSt[:, :], sbt[:, :], Iexp), [r_ph0, r_ar], [r_ph0])
    V(lambda e: e.tensor_sub(rCt[:, :], rCt[:, :], rSt[:, :]), [r_ph0], [r_ph0])
    V(lambda e: e.tensor_mul(rSt[:, :], sbt[:, :], Rexp), [r_ph0, r_ar], [r_ph0])
    V(lambda e: e.tensor_mul(rS2[:, :], cbt[:, :], Iexp), [r_ph0, r_ar], [r_ph0])
    V(lambda e: e.tensor_add(rSt[:, :], rSt[:, :], rS2[:, :]), [r_ph0], [r_ph0])
    V(lambda e: e.tensor_scalar(rSt[:, :], rSt[:, :], -1.0, None, OP.mult), [r_ph0], [r_ph0])

    for cc in range(KC):
        E("tensor", lambda e, cc=cc: e.matmul(psb[5][:, 0:KC], cosA_s[:, :],
                                              rCt[:, cc * KC:(cc + 1) * KC],
                                              start=(cc == 0), stop=False),
          reads=[r_ph0], writes=[r_ps[5]])
        E("tensor", lambda e, cc=cc: e.matmul(psb[5][:, 0:KC], sinA_s[:, :],
                                              rSt[:, cc * KC:(cc + 1) * KC],
                                              start=False, stop=(cc == KC - 1)),
          reads=[r_ph0], writes=[r_ps[5]], inc=(pe if cc == KC - 1 else None))
    E("scalar", lambda e: e.activation(ktileb[:, :], psb[5][:, 0:KC], AF.Copy),
      reads=[r_ps[5]], writes=[r_kt], inc=ac)
    E("gpsimd", lambda e: e.memset(zpref[0:1, 0:127], 0.0), writes=[r_zp], inc=po)
    dma("gpsimd", qkp, bass.AP(kpad_h, 0, [[127, 1], [1, 127]]), zpref[0:1, 0:127],
        reads=[r_zp], writes=[r_kpad])
    dma("gpsimd", qkp, bass.AP(kpad_h, 127, [[KC, 128], [1, KC]]), ktileb[:, :],
        reads=[r_kt], writes=[r_kpad])
    for b in range(128):
        dma("sync", qm, Mt[b:b + 1, :], bass.AP(kpad_h, 127 - b, [[1, 1], [1, L]]),
            reads=[r_kpad], writes=[r_M])
    ph0_done = [(pe, pe.v), (dv, dv.v), (ac, ac.v), (po, po.v)]
    pump(6)

    xq_ct = [0]
    QD_BATCH = cfg.get("qd_batch", 3)

    def emit_qdb(src, src_off, pstr, n, qc0, rsrc, xsl):
        rq = r_qsc[qc0 // 8 if qc0 < 16 else 2 + (qc0 - 16) // 2]
        if not (QD_BATCH & 1):
            for k in range(n):
                src1 = bass.AP(src, src_off + k * 128, [[pstr, 128], [1, 128]])
                E("vector", lambda e, k=k, src1=src1: e.tensor_reduce(
                    qa[:, qc0 + k:qc0 + k + 1], src1, AX.X, OP.max,
                    apply_absolute_value=True),
                  reads=[rsrc], writes=[rq], inc=dv)
                E("vector", lambda e, k=k: e.tensor_scalar(
                    qs[:, qc0 + k:qc0 + k + 1], qa[:, qc0 + k:qc0 + k + 1],
                    1e-4, 1.0 / 224.0, OP.max, OP.mult),
                  reads=[rq], writes=[rq], inc=dv)
                E("vector", lambda e, k=k: e.reciprocal(
                    qr[:, qc0 + k:qc0 + k + 1], qs[:, qc0 + k:qc0 + k + 1]),
                  reads=[rq], writes=[rq], inc=dv)
                E("vector", lambda e, k=k, src1=src1: e.tensor_scalar(
                    q8t[:, k * 128:(k + 1) * 128], src1,
                    qr[:, qc0 + k:qc0 + k + 1], None, OP.mult),
                  reads=[rsrc, rq], writes=[r_q8], inc=dv)
                E("scalar", lambda e, k=k: e.activation(
                    xqd[:, xsl * 1024 + k * 128:xsl * 1024 + (k + 1) * 128],
                    q8t[:, k * 128:(k + 1) * 128],
                    AF.Copy, scale=qs[:, qc0 + k:qc0 + k + 1]),
                  reads=[r_q8, rq], writes=[r_xq[xsl]], inc=ac)
            return
        in3 = bass.AP(src, src_off, [[pstr, 128], [128, n], [1, 128]])
        q83 = bass.AP(q8t, 0, [[8 * 128, 128], [128, n], [1, 128]])
        o3 = bass.AP(xqd, xsl * 1024, [[2 * 8 * 128, 128], [128, n], [1, 128]])
        rb3 = bass.AP(qr, qc0, [[24, 128], [1, n], [0, 128]])
        sb3 = bass.AP(qs, qc0, [[24, 128], [1, n], [0, 128]])
        E("vector", lambda e: e.tensor_reduce(qa[:, qc0:qc0 + n], in3, AX.X, OP.max,
                                              apply_absolute_value=True),
          reads=[rsrc], writes=[rq], inc=dv)
        E("vector", lambda e: e.tensor_scalar(qs[:, qc0:qc0 + n], qa[:, qc0:qc0 + n],
                                              1e-4, 1.0 / 224.0, OP.max, OP.mult),
          reads=[rq], writes=[rq], inc=dv)
        E("vector", lambda e: e.reciprocal(qr[:, qc0:qc0 + n], qs[:, qc0:qc0 + n]),
          reads=[rq], writes=[rq], inc=dv)
        E("vector", lambda e: e.tensor_tensor(q83, in3, rb3, OP.mult),
          reads=[rsrc, rq], writes=[r_q8], inc=dv)
        if QD_BATCH & 2:
            E("vector", lambda e: e.tensor_tensor(o3, q83, sb3, OP.mult),
              reads=[r_q8, rq], writes=[r_xq[xsl]], inc=dv)
        else:
            for k in range(n):
                E("scalar", lambda e, k=k: e.activation(
                    xqd[:, xsl * 1024 + k * 128:xsl * 1024 + (k + 1) * 128],
                    q8t[:, k * 128:(k + 1) * 128],
                    AF.Copy, scale=qs[:, qc0 + k:qc0 + k + 1]),
                  reads=[r_q8, rq], writes=[r_xq[xsl]], inc=ac)

    def emit_twave(src_t, src_off, n, dst_t, dst_off, dst_pstr, dst_cstr,
                   dst_res_list, rsrc, extra=()):
        for k in range(n):
            pview = psb[6][:, :].bitcast(BF16)[:, k * 128:(k + 1) * 128]
            E("tensor", lambda e, pview=pview, k=k: e.transpose(
                pview, src_t[:, src_off + k * 128:src_off + (k + 1) * 128],
                ident[:, :]),
              reads=[rsrc, r_const], writes=[r_pst[k]], inc=pe,
              extra=(extra if k == 0 else ()))
        dst3 = bass.AP(dst_t, dst_off, [[dst_pstr, 128], [dst_cstr, n], [1, 128]])
        src3 = psb[6][:, :].bitcast(BF16)[:, 0:n * 128].rearrange(
            "p (c n) -> p c n", c=n)
        E("vector", lambda e: e.tensor_copy(dst3, src3),
          reads=[r_pst[k] for k in range(n)], writes=dst_res_list, inc=dv)

    def conv_group(g):
        half = g % 2
        Ut, vts, ujs = conv_scr[half]
        r0 = g * GR
        dd_guard = ([r_Dd[half].writer] if r_Dd[half].writer else []) \
            + list(r_Dd[half].readers) + list(ph0_done)
        for sb4 in range(4):
            c0, c1 = sb4 * (KC // 4), (sb4 + 1) * (KC // 4)
            dma("gpsimd", qut[half][sb4], Ut[:, c0 * GR:c1 * GR].rearrange("p (c n) -> p c n", c=c1 - c0),
                bass.AP(u_fm, c0 * 128 * ROWS + r0,
                        [[ROWS, 128], [128 * ROWS, c1 - c0], [1, GR]]),
                writes=[r_Ut4[half][sb4]], extra=dd_guard)
        for m in range(NM):
            ms = 2 * half + (m % 2)
            rvt = r_vt[half][m % 2]
            rvs = r_vsums[ms]
            for w in range((KC + 2) // 3):
                jlo, jhi = 3 * w, min(3 * w + 3, KC)
                bank0 = 3 * (w % 2)
                bres = [r_ps[bank0 + (j - jlo)] for j in range(jlo, jhi)]
                for j in range(jlo, jhi):
                    us = j % 8
                    dma("sync", quj[us], ujs[:, us * 128:(us + 1) * 128],
                        bass.AP(u_tok, (r0 + m * 128) * D + j * 128, [[D, 128], [1, 128]]),
                        writes=[r_uj[us]], extra=dd_guard)
                for i in range(jhi):
                    for j in range(max(i, jlo), jhi):
                        bi = j - jlo
                        E("tensor",
                          lambda e, i=i, j=j, m=m, bank0=bank0, bi=bi, Ut=Ut: e.matmul(
                              psb[bank0 + bi][:, 0:128],
                              Ut[:, i * GR + m * 128:i * GR + (m + 1) * 128],
                              Mt[:, (j - i) * 128:(j - i + 1) * 128],
                              start=(i == 0), stop=(i == j)),
                          reads=[r_Ut4[half][i // (KC // 4)], r_M],
                          writes=[bres[bi]],
                          inc=(pe if i == j else None))
                pump(1)
                for j in range(jlo, jhi):
                    bi = j - jlo
                    us = j % 8
                    E("vector", lambda e, j=j, bi=bi, bank0=bank0, us=us, vt=vts[m % 2], ms=ms, ujs=ujs: e.scalar_tensor_tensor(
                        vt[:, j * 128:(j + 1) * 128], ujs[:, us * 128:(us + 1) * 128],
                        opd[:, :], psb[bank0 + bi][:, 0:128], OP.mult, OP.add,
                        accum_out=vsums[:, ms * KC + j:ms * KC + j + 1]),
                      reads=[r_uj[us], r_opd, bres[bi]], writes=[rvt, rvs], inc=dv)
                    E("scalar", lambda e, j=j, vt=vts[m % 2], ms=ms: e.activation(
                        sqscr[:, :], vt[:, j * 128:(j + 1) * 128], AF.Square,
                        accum_out=vsq[:, ms * KC + j:ms * KC + j + 1]),
                      reads=[rvt], writes=[r_sqscr, rvs], inc=ac)
        for m in range(NM):
            ms = 2 * half + (m % 2)
            rvt = r_vt[half][m % 2]
            rvs = r_vsums[ms]
            rl = r_lns[ms]
            vt = vts[m % 2]
            E("vector", lambda e, ms=ms: e.tensor_reduce(
                lnt[:, ms:ms + 1], vsums[:, ms * KC:(ms + 1) * KC], AX.X, OP.add),
              reads=[rvs], writes=[rl], inc=dv)
            E("vector", lambda e, ms=ms: e.tensor_scalar(
                lnm[:, ms:ms + 1], lnt[:, ms:ms + 1], 1.0 / D, None, OP.mult),
              reads=[rl], writes=[rl], inc=dv)
            E("vector", lambda e, ms=ms: e.tensor_reduce(
                lnt[:, ms:ms + 1], vsq[:, ms * KC:(ms + 1) * KC], AX.X, OP.add),
              reads=[rvs], writes=[rl], inc=dv)
            E("vector", lambda e, ms=ms: e.tensor_mul(
                lnt[:, 4 + ms:5 + ms], lnm[:, ms:ms + 1], lnm[:, ms:ms + 1]),
              reads=[rl], writes=[rl], inc=dv)
            E("vector", lambda e, ms=ms: e.tensor_scalar(
                lnt[:, ms:ms + 1], lnt[:, ms:ms + 1], 1.0 / D,
                lnt[:, 4 + ms:5 + ms], OP.mult, OP.subtract),
              reads=[rl], writes=[rl], inc=dv)
            E("vector", lambda e, ms=ms: e.tensor_scalar(
                lnt[:, ms:ms + 1], lnt[:, ms:ms + 1], 1e-5, None, OP.add),
              reads=[rl], writes=[rl], inc=dv)
            E("scalar", lambda e, ms=ms: e.activation(
                lnt[:, ms:ms + 1], lnt[:, ms:ms + 1], AF.Ln),
              reads=[rl], writes=[rl], inc=ac)
            E("vector", lambda e, ms=ms: e.tensor_scalar(
                lnt[:, ms:ms + 1], lnt[:, ms:ms + 1], -0.5, None, OP.mult),
              reads=[rl], writes=[rl], inc=dv)
            E("scalar", lambda e, ms=ms: e.activation(
                lnr[:, ms:ms + 1], lnt[:, ms:ms + 1], AF.Exp),
              reads=[rl], writes=[rl], inc=ac)
            E("vector", lambda e, ms=ms, vt=vt: e.tensor_scalar(
                vt[:, :], vt[:, :], lnm[:, ms:ms + 1], lnr[:, ms:ms + 1],
                OP.subtract, OP.mult),
              reads=[rvt, rl], writes=[rvt], inc=dv)
            x1sl = (g * NM + m) % 2
            dma("gpsimd", qx1w[x1sl],
                bass.AP(x1_h, (r0 + m * 128) * D, [[D, 128], [1, D]]), vt[:, :],
                reads=[rvt], writes=[r_x1h[x1sl]])
            for b in range((KC + 7) // 8):
                n = min(8, KC - 8 * b)
                xsl = xq_ct[0] % 2
                xq_ct[0] += 1
                emit_qdb(vt, b * 8 * 128, D, n, 8 * xsl, rvt, xsl)
                emit_twave(xqd, xsl * 1024, n,
                           XdH[half], b * 8 * GR + m * 128, KC * GR, GR,
                           [r_Xd[half][8 * b + k] for k in range(n)], r_xq[xsl],
                           extra=(ph0_done if (g == 0 and m == 0 and b == 0) else ()))

    def ffn1_pair(pair):
        g0 = pair * NPAIR
        esb_guard = list(pair_guard[0])
        dsl_ct = [0]
        for it in range(NIT):
            sl = it % 2
            need_ib = (it + 1) * IT // 128
            while wq_pos[0] < min(need_ib * (D // WQK0), len(wq_units)):
                wq_step()
            ready = [r for ib in range(it * IT // 128, (it + 1) * IT // 128)
                     for ks in range(D // WQK0) for r in (w0rdy[ib][ks] or [])]
            dma("sync", qw0[sl],
                W0b[:, sl * KC * IT:(sl + 1) * KC * IT].rearrange("p (c n) -> p c n", c=KC),
                bass.AP(w0dq, it * IT, [[I, 128], [128 * I, KC], [1, IT]]),
                writes=[r_W0[sl]], extra=(ready + esb_guard))
            dma("sync", qb0[sl], b0b[0:1, sl * IT:(sl + 1) * IT],
                bass.AP(b0h, it * IT, [[IT, 1], [1, IT]]), writes=[r_b0[sl]])
            for gg in range(NPAIR):
                half = (g0 + gg) % 2
                dd = DdA if half == 0 else DdB
                rdd = r_Dd[half]
                for m in range(NM):
                    bank = 2 * gg + m
                    rb = r_ps[bank]
                    E("tensor", lambda e, sl=sl, bank=bank: e.matmul(
                        psb[bank][:, 0:IT], onesb[0:1, :], b0b[0:1, sl * IT:(sl + 1) * IT],
                        start=True, stop=False),
                      reads=[r_b0[sl], r_const], writes=[rb])
                    for cc in range(KC):
                        E("tensor", lambda e, sl=sl, bank=bank, cc=cc, m=m, half=half: e.matmul(
                            psb[bank][:, 0:IT],
                            XdH[half][:, cc * GR + m * 128:cc * GR + (m + 1) * 128],
                            W0b[:, sl * KC * IT + cc * IT:sl * KC * IT + (cc + 1) * IT],
                            start=False, stop=(cc == KC - 1)),
                          reads=[r_Xd[half][cc], r_W0[sl], r_b0[sl]], writes=[rb],
                          inc=(pe if cc == KC - 1 or cc == KC // 2 else None))
                    dsl = dsl_ct[0] % 2
                    dsl_ct[0] += 1
                    E("scalar", lambda e, bank=bank, dsl=dsl: e.activation(
                        dgt[:, dsl * IT:(dsl + 1) * IT], psb[bank][:, 0:IT], AF.Gelu),
                      reads=[rb], writes=[r_dg[dsl]], inc=ac)
                    xsl = xq_ct[0] % 2
                    xq_ct[0] += 1
                    emit_qdb(dgt, dsl * IT, 2 * IT, ITC, 16 + 2 * dsl, r_dg[dsl], xsl)
                    emit_twave(xqd, xsl * 1024, ITC,
                               dd, it * ITC * GR + m * 128, IC * GR, GR,
                               [rdd], r_xq[xsl])

    def ffn2_pair(pair):
        while wq_pos[0] < len(wq_units):
            wq_step()
        wq_done = ([(dv, dv.v), (po, po.v), (ac, ac.v)]
                   + [(q, q.v) for q in qqs] + [(q, q.v) for q in qqi])
        wq_done_g[0] = wq_done
        first_w1 = [True]
        g0 = pair * NPAIR
        NHP = (NHQ + 1) // 2
        for hp in range(NHP):
            hqs = list(range(2 * hp, min(2 * hp + 2, NHQ)))
            nh = len(hqs)
            for hi in range(nh):
                dma("sync", qb1, b1b[0:1, :],
                    bass.AP(b1h, hqs[hi] * HQ, [[HQ, 1], [1, HQ]]),
                    writes=[r_b1])
                for mm in range(NMM):
                    bank = mm + 4 * hi
                    E("tensor", lambda e, bank=bank: e.matmul(
                        psb[bank][:, 0:HQ], onesb[0:1, :], b1b[0:1, :],
                        start=True, stop=False),
                      reads=[r_b1, r_const], writes=[r_ps[bank]],
                      inc=(pe if mm == NMM - 1 else None),
                      extra=(fence() if hp == 0 and mm == 0 and hi == 0 else ()))
            for blk in range(IC // W1B):
                wsl = blk % 2
                ready = []
                for hq in hqs:
                    for j in range(HQ // 128):
                        hb = (hq * HQ) // 128 + j
                        for kk in range(W1B):
                            ic = blk * W1B + kk
                            ready.extend(w1rdy[hb][ic * 128 // WQK1] or [])
                dma("sync", qw1[wsl],
                    W1b[:, wsl * W1B * nh * HQ:wsl * W1B * nh * HQ + W1B * nh * HQ]
                    .rearrange("p (c n) -> p c n", c=W1B),
                    bass.AP(w1dq, blk * W1B * 128 * D + hqs[0] * HQ,
                            [[D, 128], [128 * D, W1B], [1, nh * HQ]]),
                    writes=[r_W1[wsl]],
                    extra=(ready + (wq_done if first_w1[0] else [])))
                first_w1[0] = False
                for kk in range(W1B):
                    ic = blk * W1B + kk
                    for mm in range(NMM):
                        gg = mm // NM
                        m = mm % NM
                        ddh = (g0 + gg) % 2
                        ddt = DdA if ddh == 0 else DdB
                        for hi in range(nh):
                            bank = mm + 4 * hi
                            last = (blk == IC // W1B - 1 and kk == W1B - 1)
                            blk_end = (kk == W1B - 1 and mm == NMM - 1 and hi == nh - 1)
                            E("tensor", lambda e, ddt=ddt, ic=ic, m=m, wsl=wsl, kk=kk, hi=hi, nh=nh, bank=bank, last=last: e.matmul(
                                psb[bank][:, 0:HQ],
                                ddt[:, ic * GR + m * 128:ic * GR + (m + 1) * 128],
                                W1b[:, wsl * W1B * nh * HQ + (kk * nh + hi) * HQ:
                                     wsl * W1B * nh * HQ + (kk * nh + hi + 1) * HQ],
                                start=False, stop=last),
                              reads=[r_Dd[ddh], r_W1[wsl]],
                              writes=[r_ps[bank]],
                              inc=(pe if (last or blk_end) else None))
            for mm in range(NMM):
                for hi, hq in enumerate(hqs):
                    bank = mm + 4 * hi
                    if (mm + hi) % 2 == 0:
                        E("scalar", lambda e, mm=mm, bank=bank, hq=hq: e.activation(
                            Esb[:, mm * D + hq * HQ:mm * D + (hq + 1) * HQ],
                            psb[bank][:, 0:HQ], AF.Copy),
                          reads=[r_ps[bank]], writes=[r_Esb[mm]], inc=ac)
                    else:
                        E("vector", lambda e, mm=mm, bank=bank, hq=hq: e.tensor_copy(
                            Esb[:, mm * D + hq * HQ:mm * D + (hq + 1) * HQ],
                            psb[bank][:, 0:HQ]),
                          reads=[r_ps[bank]], writes=[r_Esb[mm]], inc=dv)

    def ln2_pair(pair):
        g0 = pair * NPAIR
        r0 = g0 * GR
        x1done = [(qx1w[0], qx1w[0].v), (qx1w[1], qx1w[1].v)] + list(wq_done_g[0])
        for mm in range(NMM):
            rows = r0 + mm * 128
            dma("sync", qx1r, X1rb[:, :], bass.AP(x1_h, rows * D, [[D, 128], [1, D]]),
                reads=[r_x1h[0], r_x1h[1]], writes=[r_x1rb], extra=x1done)
            E("vector", lambda e, mm=mm: e.scalar_tensor_tensor(
                v2t[:, :], Esb[:, mm * D:(mm + 1) * D], 1.0, X1rb[:, :],
                OP.mult, OP.add, accum_out=l2s[:, mm:mm + 1]),
              reads=[r_Esb[mm], r_x1rb], writes=list(r_Xd[0]) + [r_l2], inc=dv)
            for h8 in range(D // 512):
                E("scalar", lambda e, mm=mm, h8=h8: e.activation(
                    psb[mm][:, 0:512], v2t[:, h8 * 512:(h8 + 1) * 512], AF.Square,
                    accum_out=l2q[:, mm * 8 + h8:mm * 8 + h8 + 1]),
                  reads=list(r_Xd[0]), writes=[r_ps[mm], r_l2], inc=ac)
            E("vector", lambda e, mm=mm: e.tensor_reduce(
                l2m[:, mm:mm + 1], l2q[:, mm * 8:mm * 8 + D // 512], AX.X, OP.add),
              reads=[r_l2], writes=[r_l2], inc=dv)
            E("vector", lambda e, mm=mm: e.tensor_scalar(
                l2s[:, mm:mm + 1], l2s[:, mm:mm + 1], 1.0 / D, None, OP.mult),
              reads=[r_l2], writes=[r_l2], inc=dv)
            E("vector", lambda e, mm=mm: e.tensor_mul(
                l2r[:, mm:mm + 1], l2s[:, mm:mm + 1], l2s[:, mm:mm + 1]),
              reads=[r_l2], writes=[r_l2], inc=dv)
            E("vector", lambda e, mm=mm: e.tensor_scalar(
                l2m[:, mm:mm + 1], l2m[:, mm:mm + 1], 1.0 / D, l2r[:, mm:mm + 1],
                OP.mult, OP.subtract),
              reads=[r_l2], writes=[r_l2], inc=dv)
            E("vector", lambda e, mm=mm: e.tensor_scalar(
                l2m[:, mm:mm + 1], l2m[:, mm:mm + 1], 1e-5, None, OP.add),
              reads=[r_l2], writes=[r_l2], inc=dv)
            E("scalar", lambda e, mm=mm: e.activation(
                l2r[:, mm:mm + 1], l2m[:, mm:mm + 1], AF.Ln),
              reads=[r_l2], writes=[r_l2], inc=ac)
            E("vector", lambda e, mm=mm: e.tensor_scalar(
                l2r[:, mm:mm + 1], l2r[:, mm:mm + 1], -0.5, None, OP.mult),
              reads=[r_l2], writes=[r_l2], inc=dv)
            E("scalar", lambda e, mm=mm: e.activation(
                l2r[:, mm:mm + 1], l2r[:, mm:mm + 1], AF.Exp),
              reads=[r_l2], writes=[r_l2], inc=ac)
            E("vector", lambda e, mm=mm: e.tensor_scalar(
                v2t[:, :], v2t[:, :], l2s[:, mm:mm + 1], l2r[:, mm:mm + 1],
                OP.subtract, OP.mult),
              reads=[r_l2] + list(r_Xd[0]), writes=list(r_Xd[0]), inc=dv)
            dma("gpsimd", qout[mm % 2], bass.AP(out_h, rows * D, [[D, 128], [1, D]]),
                v2t[:, :], reads=list(r_Xd[0]), writes=[r_outh[mm % 2]])

    wq_done_g = [[]]
    pair_guard = [[]]
    for pair in range(NG // NPAIR):
        for gg in range(NPAIR):
            conv_group(pair * NPAIR + gg)
        ffn1_pair(pair)
        pump(8)
        ffn2_pair(pair)
        ln2_pair(pair)
        pair_guard[0] = [(dv, dv.v), (ac, ac.v)]

    E("gpsimd", lambda e: e.nop(),
      extra=[(qout[0], qout[0].v), (qout[1], qout[1].v)], inc=po)

    import contextlib
    stack = contextlib.ExitStack()
    for c in ctrs:
        c.sem = stack.enter_context(nc.semaphore(c.name))
    with nc.Block() as block:
        @block.sync
        def _(sync):
            plan.replay("sync", sync)

        @block.gpsimd
        def _(gpsimd):
            plan.replay("gpsimd", gpsimd)

        @block.tensor
        def _(tensor):
            plan.replay("tensor", tensor)

        @block.vector
        def _(vector):
            plan.replay("vector", vector)

        @block.scalar
        def _(scalar):
            plan.replay("scalar", scalar)
    stack.close()
    return nc


def make_core_inputs(ins, rows_slice, cfg):
    r0, r1 = rows_slice
    D = cfg["d"]
    u = ins["L0_attn_u"].reshape(-1, D)
    lam = ins["L0_attn_Lambda_ri"]
    pv = ins["L0_attn_p_ri"]
    qv = ins["L0_attn_q_ri"]
    vc = ins["L0_attn_Vc_ri"]
    ct = ins["L0_attn_Ct_ri"].reshape(-1, 2)
    return {
        "u_tok": np.ascontiguousarray(u[r0:r1]),
        "u_fm": np.ascontiguousarray(u[r0:r1].T),
        "w0n": np.ascontiguousarray(ins["L0_ffn_W0"]),
        "w1n": np.ascontiguousarray(ins["L0_ffn_W1"]),
        "b0h": np.ascontiguousarray(ins["L0_ffn_bias0"].reshape(1, -1)),
        "b1h": np.ascontiguousarray(ins["L0_ffn_bias1"].reshape(1, -1)),
        "lam_re": np.ascontiguousarray(lam[:, 0:1]),
        "lam_im": np.ascontiguousarray(lam[:, 1:2]),
        "pv_re": np.ascontiguousarray(pv[:, 0:1]),
        "pv_im": np.ascontiguousarray(pv[:, 1:2]),
        "qv_re": np.ascontiguousarray(qv[:, 0:1]),
        "qv_im": np.ascontiguousarray(qv[:, 1:2]),
        "ct_re": np.ascontiguousarray(ct[:, 0:1]),
        "ct_im": np.ascontiguousarray(ct[:, 1:2]),
        "vct_re": np.ascontiguousarray(vc[:, :, 0].T),
        "vct_im": np.ascontiguousarray(vc[:, :, 1].T),
        "bp": np.ascontiguousarray(ins["L0_attn_Bp"]),
        "dcoef": np.ascontiguousarray(ins["L0_attn_D"].reshape(1, 1)),
        "logstep": np.ascontiguousarray(ins["L0_attn_log_step"].reshape(1, 1)),
    }


CFG = dict(rows=1024, d=4096, ffn_i=14336, group_rows=256, pair_groups=2,
           i_tile=256, w1_block=1)
N_CORES = 8


def kernel(**inputs):
    from concourse.bass_utils import run_bass_kernel_spmd
    rows_total = inputs["L0_attn_u"].shape[0] * inputs["L0_attn_u"].shape[1]
    rpc = rows_total // N_CORES
    nc = build(CFG)
    in_maps = [make_core_inputs(inputs, (c * rpc, (c + 1) * rpc), CFG)
               for c in range(N_CORES)]
    res = run_bass_kernel_spmd(nc, in_maps, core_ids=list(range(N_CORES)))
    out = np.concatenate([np.asarray(r["out"]) for r in res.results], axis=0)
    return out.reshape(inputs["L0_attn_u"].shape).astype(np.float32)
